# version 14
# baseline (speedup 1.0000x reference)
import contextlib
import numpy as np
import concourse.bass as bass
import concourse.mybir as mybir
from concourse.bass_utils import run_bass_kernel_spmd

F32 = mybir.dt.float32
BF16 = mybir.dt.bfloat16
AF = mybir.ActivationFunctionType
ALU = mybir.AluOpType
AX = mybir.AxisListType

D = 1024
NKC = 8
H = 16
NH = 64
QC = 128
TT = 512
NW = TT // QC
NE = 8
HD = 512
EPS = 1e-5


def bc_last(ap, n):
    return bass.AP(ap.tensor, ap.offset, list(ap.ap[:-1]) + [[0, n]])


def bc_mid(ap, n):
    return bass.AP(ap.tensor, ap.offset, [ap.ap[0], [0, n]] + list(ap.ap[1:]))


class Trk:

    def __init__(self):
        self.cnt = {}
        self.ev = {}
        self.auto = set()
        self.pass2 = False
        self.wlog = {}
        self.cur = None

    def inc(self, sem, instr, key=None, step=1):
        if sem.num in self.auto:
            if key is not None:
                self.ev[key] = (sem, self.cnt.get(sem.num, 0))
            return instr
        c = self.cnt.get(sem.num, 0) + step
        self.cnt[sem.num] = c
        instr.then_inc(sem, step)
        if key is not None:
            self.ev[key] = (sem, c)
        return instr

    def mark(self, sem, key):
        self.ev[key] = (sem, self.cnt.get(sem.num, 0))

    def wait(self, eng, key):
        if self.pass2:
            w = self.wlog[self.cur].pop(0)
            if w is not None:
                eng.wait_ge(w[0], w[1])
            return
        sem, v = self.ev[key]
        self.wlog.setdefault(self.cur, []).append((sem, v))
        eng.wait_ge(sem, v)

    def wait_opt(self, eng, key):
        if self.pass2:
            self.wait(eng, None)
            return
        if key is not None:
            self.wait(eng, key)
        else:
            self.wlog.setdefault(self.cur, []).append(None)


class MockInstr:
    def then_inc(self, sem, step=1):
        return self


class MockEng:
    def wait_ge(self, sem, val):
        return MockInstr()

    def __getattr__(self, nm):
        def f(*a, **k):
            return MockInstr()
        return f


_SENG_EXEMPT = {"wait_ge", "dma_start", "print", "then_inc", "engine",
                "collective_compute", "breakpoint"}


class SEng:

    def __init__(self, eng, sem, tk):
        self._eng = eng
        self._sem = sem
        self._tk = tk

    def __getattr__(self, nm):
        a = getattr(self._eng, nm)
        if nm.startswith("_") or nm in _SENG_EXEMPT or not callable(a):
            return a

        def wrap(*args, **kw):
            c = self._tk.cnt.get(self._sem.num, 0)
            if c:
                self._eng.wait_ge(self._sem, c)
            instr = a(*args, **kw)
            self._tk.cnt[self._sem.num] = c + 1
            instr.then_inc(self._sem, 1)
            return instr
        return wrap


def build_nc(T_total, debug_stage=3):
    n_tiles = T_total // TT
    nc = bass.Bass("TRN2", target_bir_lowering=False)
    tk = Trk()

    x_d = nc.dram_tensor("x", [D, T_total + TT], F32, kind="ExternalInput").ap()
    out_d = nc.dram_tensor("out", [D, T_total], F32, kind="ExternalOutput").ap()
    flag_d = nc.dram_tensor("flag", [128, 1], F32, kind="ExternalInput").ap()
    spx1_d = nc.dram_tensor("spx1", [D, T_total], F32).ap()
    spt1_d = nc.dram_tensor("spt1", [D, T_total + 1], BF16).ap()
    spnm_d = nc.dram_tensor("spnm", [1, T_total + 1], BF16).ap()
    spkk_d = nc.dram_tensor("spkk", [D, T_total], BF16).ap()
    spvv_d = nc.dram_tensor("spvv", [T_total, D], BF16).ap()
    spst_d = nc.dram_tensor("spst", [T_total // QC, 128, 512], F32).ap()
    ccin_d = nc.dram_tensor("ccin", [128, 512], F32)
    ccout_d = nc.dram_tensor("ccout", [2, 128, 512], F32)
    wr_d = nc.dram_tensor("WrT", [D, D], F32, kind="ExternalInput").ap()
    wk_d = nc.dram_tensor("WkT", [D, D], F32, kind="ExternalInput").ap()
    wv_d = nc.dram_tensor("WvT", [D, D], F32, kind="ExternalInput").ap()
    wo_d = nc.dram_tensor("WoT", [D, D], F32, kind="ExternalInput").ap()
    wd_d = nc.dram_tensor("WdT", [D, HD], F32, kind="ExternalInput").ap()
    wu_d = nc.dram_tensor("WuT", [HD, D], F32, kind="ExternalInput").ap()
    wg_d = nc.dram_tensor("wgate", [D, NE], F32, kind="ExternalInput").ap()
    bu_d = nc.dram_tensor("bu", [NE, D], F32, kind="ExternalInput").ap()
    bd_d = nc.dram_tensor("bd", [HD], F32, kind="ExternalInput").ap()
    br_d = nc.dram_tensor("br", [D], F32, kind="ExternalInput").ap()
    bk_d = nc.dram_tensor("bk", [D], F32, kind="ExternalInput").ap()
    bv_d = nc.dram_tensor("bv", [D], F32, kind="ExternalInput").ap()
    bo_d = nc.dram_tensor("bo", [D], F32, kind="ExternalInput").ap()
    mk_d = nc.dram_tensor("maak", [D], F32, kind="ExternalInput").ap()
    mv_d = nc.dram_tensor("maav", [D], F32, kind="ExternalInput").ap()
    mr_d = nc.dram_tensor("maar", [D], F32, kind="ExternalInput").ap()
    td_d = nc.dram_tensor("tdecay", [H], F32, kind="ExternalInput").ap()
    tf_d = nc.dram_tensor("tfaaaa", [H], F32, kind="ExternalInput").ap()

    ctx = contextlib.ExitStack()
    sb = lambda name, shape, dt: ctx.enter_context(nc.sbuf_tensor(name, shape, dt))
    ps = lambda name, shape: ctx.enter_context(nc.psum_tensor(name, shape, F32))
    sem = lambda name: ctx.enter_context(nc.semaphore(name))

    wr_s = sb("wr_s", [128, NKC, D], BF16)
    wk_s = sb("wk_s", [128, NKC, D], BF16)
    wv_s = sb("wv_s", [128, NKC, D], BF16)
    wo_s = sb("wo_s", [128, NKC, D], BF16)
    wd_s = sb("wd_s", [128, NKC, HD], BF16)
    wu_s = sb("wu_s", [128, 4, D], BF16)
    wg_s = sb("wg_s", [128, NKC, NE + 1], BF16)
    bu_s = sb("bu_s", [NE, D], BF16)
    bd_s = sb("bd_s", [128, 4], F32)
    bo_s = sb("bo_s", [128, NKC], F32)
    bvb_s = sb("bvb_s", [128, D], BF16)
    mkb_s = sb("mkb_s", [128, NKC], BF16)
    mvb_s = sb("mvb_s", [128, NKC], BF16)
    mrb_s = sb("mrb_s", [128, NKC], BF16)
    omkb_s = sb("omkb_s", [128, NKC], BF16)
    omvb_s = sb("omvb_s", [128, NKC], BF16)
    omrb_s = sb("omrb_s", [128, NKC], BF16)
    s12_s = {"k": sb("s12k_s", [2, D], BF16), "v": sb("s12v_s", [2, D], BF16),
             "r": sb("s12r_s", [2, D], BF16)}
    wdsum_s = sb("wdsum_s", [1, HD], BF16)
    bkr_s = sb("bkr_s", [1, D], BF16)
    brr_s = sb("brr_s", [1, D], BF16)
    onestt_s = sb("onestt_s", [1, TT], BF16)
    wgsb_s = sb("wgsb_s", [128, NE], F32)
    ind_s = sb("ind_s", [NE, 4, 128], BF16)
    identf = sb("identf", [128, 128], F32)
    identb = sb("identb", [128, 128], BF16)
    onesf = sb("onesf", [1, 128], F32)
    onesbr = sb("onesbr", [1, 128], BF16)
    onesb = sb("onesb", [128, 1], BF16)
    e01b = sb("e01b", [1, 4], BF16)
    wmt_s = sb("wmt_s", [128, H, QC], BF16)
    wbp_s = sb("wbp_s", [128, NE, QC], BF16)
    wktm_s = sb("wktm_s", [128, D], BF16)
    wsp_s = sb("wsp_s", [128, NE], F32)
    zb_s = sb("zb_s", [128, H], F32)
    ub_s = sb("ub_s", [128, H], F32)



    xres_s = sb("xres_s", [128, 2, TT], F32)
    x1_s = sb("x1_s", [128, NKC, TT], F32)
    scr2_s = sb("scr2_s", [128, 512], F32)
    iof_s = scr2_s[:, 0:128]
    iog_s = scr2_s[:, 128:256]
    iod_s = scr2_s[:, 256:384]
    iorow_s = sb("iorow_s", [NE, 2, 64], F32)
    ioh_s = [iorow_s[:, :, :]] * 4
    rowa_s = sb("rowa_s", [1, 16], F32).ap()
    rowb_s = sb("rowb_s", [1, 16], F32).ap()
    nhalf_s = sb("nhalf_s", [1, 1], F32)
    xbf_s = sb("xbf_s", [128, NKC, TT], BF16)
    xsq_s = sb("xsq_s", [128, NKC, TT], BF16)
    xv_s = sb("xv_s", [128, NKC, TT], BF16)
    t1_s = sb("t1_s", [128, NKC, TT + 1], BF16)
    kk_s = sb("kk_s", [128, NKC, TT], BF16)
    rr_s = sb("rr_s", [128, NKC, TT], BF16)
    ktm2_s = rr_s.rearrange("p a b -> p (a b)")[:, 0:D]
    vv_s = sb("vv_s", [128, NW, D], BF16)
    ktm_s = sb("ktm_s", [128, D], BF16)
    kwm_s = ktm_s
    scb_s = ktm_s[:, 0:TT]
    rrw_s = ktm_s[:, :].rearrange("p (a b) -> p a b", a=NE)
    atw_s = sb("atw_s", [128, 2, 2 * QC], BF16)
    st_s = sb("st_s", [128, 2, NE * NH], F32)
    stb_s = sb("stb_s", [128, 2, NE * NH], BF16)
    sigma_s = xres_s[:, 0, :]
    sigd_s = xres_s[:, 1, :]
    flag_s = sb("flag_s", [128, 1], F32)
    gh_s = xv_s
    rb_s = sb("rb_s", [128, TT], F32)
    gfm_s = sb("gfm_s", [NE, TT], BF16)
    gs9_s = scr2_s[0:NE + 1, 0:TT]
    lt64_s = sb("lt64_s", [128, 36], BF16)
    rst8_s = sb("rst8_s", [128, 8], F32)
    gtm_s = sb("gtm_s", [128, 4, NE], F32)
    gw1_s = sb("gw1_s", [128, 4, NE], F32)
    gm1_s = sb("gm1_s", [128, 4, 1], F32)
    gm2_s = sb("gm2_s", [128, 4, 1], F32)
    gmk1_s = sb("gmk1_s", [128, 4, NE], BF16)
    gmk2_s = sb("gmk2_s", [128, 4, NE], BF16)
    gde_s = sb("gde_s", [128, 4, 1], F32)
    gg1_s = sb("gg1_s", [128, 4, 1], F32)
    gg2_s = sb("gg2_s", [128, 4, 1], F32)
    mrow_s = sb("mrow_s", [1, TT], F32)
    trow_s = sb("trow_s", [1, TT], F32)
    rsg_s = sb("rsg_s", [1, TT], F32)

    ngmr_s = sb("ngmr_s", [1, TT + 1], BF16)
    nm_s = sb("nm_s", [2, TT], BF16)
    negm_s = sb("negm_s", [1, TT], BF16).ap()

    PB = [ps(f"PB{k}", [128, 512]) for k in range(8)]
    p_proj = [PB[0].ap(), PB[1].ap(), PB[6].ap(), PB[7].ap()]
    p_gate = PB[2][0:NE + 1, :]
    p_row = PB[3][0:2, :]
    p_bc = PB[4].ap()
    p_sm = PB[5][:, 0:128]

    pd_sl = [PB[2][:, 0:NH], PB[3][:, 0:NH]]
    py_sl = [PB[4][:, 0:QC], PB[5][:, 0:QC]]

    sT = sem("sT")
    sV = sem("sV")
    sS = sem("sS")
    sG = sem("sG")
    dX = sem("dX")
    dXa = sem("dXa")
    dXb = sem("dXb")
    dW = sem("dW")
    dK = sem("dK")
    dK2 = sem("dK2")
    dKb = sem("dKb")
    dKb2 = sem("dKb2")
    dO = sem("dO")
    dSx = sem("dSx")
    dSt = sem("dSt")
    dSk = sem("dSk")
    dSv = sem("dSv")
    dSs = [sem("dSsa"), sem("dSsb")]
    dLx = sem("dLx")
    dLt = sem("dLt")
    dLn = sem("dLn")
    dLk = sem("dLk")
    dLv = sem("dLv")
    dLs = [sem("dLsa"), sem("dLsb")]
    dC = sem("dC")

    tk.auto.update({sV.num, sS.num, sG.num})
    EMITS = []
    FIDX = [0]
    mock = MockEng()
    mock_s = SEng(mock, sS, tk)
    mock_v = SEng(mock, sV, tk)
    mock_g = SEng(mock, sG, tk)

    def _reg(key, f, wrap):
        idx = FIDX[0]
        FIDX[0] += 1
        EMITS.append((key, idx, f))
        tk.cur = idx
        f(wrap)

    def SY(f):
        _reg("y", f, mock)

    def TE(f):
        _reg("t", f, mock)

    def SC(f):
        _reg("s", f, mock_s)

    def VE(f):
        _reg("v", f, mock_v)

    def GP(f):
        _reg("g", f, mock_g)

    free = {"xbf": None, "xsq": None, "p_gate": None, "p_row": None, "p_bc": None,
            "p_sm": None, "pp": [None, None], "pw": [None, None],
            "pwy": [None, None], "pwd": [None, None],
            "ps4": [None, None, None, None],
            "negm": None, "rsg": None, "trow": None,
            "nm": None, "ngmr": None, "r2s": None, "gs": None, "gfm": None,
            "gtm": None, "hb": [None, None]}

    def _sy(sync):
        for d, s in ((bd_d.rearrange("(c p) -> p c", p=128), bd_s),
                     (bo_d.rearrange("(c p) -> p c", p=128), bo_s)):
            tk.inc(dX, sync.dma_start(out=s[:, :], in_=d,
                                      allow_slow_non_contiguous=True), step=16)
        tk.inc(dX, sync.dma_start(out=rowa_s, in_=td_d[None, :]), "td_row", 16)
        tk.inc(dX, sync.dma_start(out=rowb_s, in_=tf_d[None, :]), "tf_row", 16)
        tk.inc(dX, sync.dma_start(out=flag_s[:, :], in_=flag_d), "flag", 16)
        for j, dsem in ((0, dXa), (1, dXb)):
            tk.inc(dsem, sync.dma_start(
                out=xres_s[:, j, :],
                in_=x_d.rearrange("(c p) T -> p c T", p=128)[:, j, 0:TT]),
                f"xr:-1:{j}", 16)
        tk.mark(dX, "smalls")
        tk.ev["td_row"] = tk.ev["smalls"]
        tk.ev["tf_row"] = tk.ev["smalls"]
    SY(_sy)

    def _gw(g):
        for d, s in ((wr_d, wr_s), (wk_d, wk_s), (wv_d, wv_s), (wo_d, wo_s),
                     (wd_d, wd_s), (wu_d, wu_s)):
            tk.inc(dW, g.dma_start(out=s[:, :, :],
                                   in_=d.rearrange("(c p) o -> p c o", p=128)), step=16)
        tk.inc(dW, g.dma_start(out=wg_s[:, :, 1:NE + 1],
                               in_=wg_d.rearrange("(c p) o -> p c o", p=128)), step=16)
        tk.inc(dW, g.dma_start(out=bu_s[:, :], in_=bu_d), step=16)
        tk.inc(dW, g.dma_start(out=mkb_s[:, :], in_=mk_d.rearrange("(c p) -> p c", p=128), allow_slow_non_contiguous=True), step=16)
        tk.inc(dW, g.dma_start(out=mvb_s[:, :], in_=mv_d.rearrange("(c p) -> p c", p=128), allow_slow_non_contiguous=True), step=16)
        tk.inc(dW, g.dma_start(out=mrb_s[:, :], in_=mr_d.rearrange("(c p) -> p c", p=128), allow_slow_non_contiguous=True), step=16)
        tk.inc(dW, g.dma_start(out=bvb_s[0:1, :], in_=bv_d[None, :]), "bv_row", 16)
        tk.inc(dW, g.dma_start(out=bkr_s[:, :], in_=bk_d[None, :]), step=16)
        tk.inc(dW, g.dma_start(out=brr_s[:, :], in_=br_d[None, :]), step=16)
        tk.inc(dW, g.dma_start(
            out=xbf_s[:, :, :],
            in_=x_d.rearrange("(c p) T -> p c T", p=128)[:, :, 0:TT]), "f1_cast:-1", 16)
        tk.mark(dW, "weights")
        tk.ev["bv_row"] = tk.ev["weights"]
        tk.inc(sG, g.iota(out=iof_s, pattern=[[1, 128]], base=-1,
                          channel_multiplier=-1, allow_small_or_imprecise_dtypes=True),
               "io_d1")
        tk.inc(sG, g.tensor_scalar(out=iod_s, in0=iof_s, scalar1=0.0, scalar2=None,
                                   op0=ALU.max), "io_d1c")
        tk.inc(sG, g.iota(out=iog_s, pattern=[[1, 128]], base=0,
                          channel_multiplier=0, allow_small_or_imprecise_dtypes=True),
               "io_fr")
        for m in range(4):
            if m > 0:
                tk.wait(g, f"ind{m - 1}")
            tk.inc(sG, g.iota(out=ioh_s[m], pattern=[[-1, 2], [0, 64]],
                              base=-2 * m, channel_multiplier=1,
                              allow_small_or_imprecise_dtypes=True), f"io_ind{m}")
            tk.wait(g, f"io_ind{m}")
            tk.inc(sG, g.tensor_scalar(out=ind_s[:, m, :].rearrange("p (a b) -> p a b", a=2),
                                       in0=ioh_s[m], scalar1=0.0, scalar2=None,
                                       op0=ALU.is_equal), f"ind{m}")
    GP(_gw)

    def _v1(v):
        tk.wait(v, "io_d1")
        tk.inc(sV, v.tensor_scalar(out=identf[:, :], in0=iof_s, scalar1=-1.0,
                                   scalar2=None, op0=ALU.is_equal), "identf")
        tk.inc(sV, v.tensor_scalar(out=identb[:, :], in0=iof_s, scalar1=-1.0,
                                   scalar2=None, op0=ALU.is_equal), "identb")
        v.memset(onesf[:, :], 1.0)
        v.memset(onesbr[:, :], 1.0)
        v.memset(onesb[:, :], 1.0)
        v.memset(e01b[:, :], 0.0)
        v.memset(e01b[0:1, 0:1], 1.0)
        v.memset(e01b[0:1, 3:4], 1.0)
        v.memset(onestt_s[:, :], 1.0)
        v.memset(nhalf_s[:, :], -0.5)
        v.memset(st_s[:, :, :], 0.0)
        v.memset(stb_s[:, :, :], 0.0)
        v.memset(t1_s[:, :, 0:1], 0.0)
        tk.inc(sV, v.memset(ngmr_s[:, :], 0.0), "consts")
        tk.wait(v, "weights")
        v.memset(wg_s[:, :, 0:1], 1.0)
        v.tensor_scalar(out=omkb_s[:, :], in0=mkb_s[:, :], scalar1=-1.0, scalar2=1.0,
                        op0=ALU.mult, op1=ALU.add)
        v.tensor_scalar(out=omvb_s[:, :], in0=mvb_s[:, :], scalar1=-1.0, scalar2=1.0,
                        op0=ALU.mult, op1=ALU.add)
        tk.inc(sV, v.tensor_scalar(out=omrb_s[:, :], in0=mrb_s[:, :], scalar1=-1.0,
                                   scalar2=1.0, op0=ALU.mult, op1=ALU.add), "omaa")
    VE(_v1)

    def _s1(s):
        tk.wait(s, "td_row")
        tk.inc(sS, s.activation(out=rowa_s, in_=rowa_s, func=AF.Exp),
               "exp_td")
    SC(_s1)

    def _v2(v):
        tk.wait(v, "exp_td")
        tk.inc(sV, v.tensor_scalar(out=rowa_s, in0=rowa_s, scalar1=-1.0,
                                   scalar2=None, op0=ALU.mult), "z_row")
    VE(_v2)

    def _t1(t):
        tk.wait(t, "z_row")
        tk.wait(t, "tf_row")
        t.matmul(p_sm[:, 0:16], onesf[:, :], rowa_s, start=True, stop=True)
        tk.inc(sT, t.matmul(p_sm[:, 16:32], onesf[:, :], rowb_s, start=True,
                            stop=True), "zbub_mm")
    TE(_t1)

    def _v3(v):
        tk.wait(v, "zbub_mm")
        tk.inc(sV, v.tensor_copy(out=zb_s[:, :], in_=p_sm[:, 0:16]), "zb")
        tk.inc(sV, v.tensor_copy(out=ub_s[:, :], in_=p_sm[:, 16:32]), "ub")
    VE(_v3)

    for h in range(H):
        scr = rb_s[:, (h % 2) * QC:(h % 2) * QC + QC]
        def _se(s, h=h, scr=scr):
            if h == 0:
                tk.wait(s, "zb")
            if h >= 2:
                tk.wait(s, f"wmt{h - 2}")
            if h == 0:
                tk.wait(s, "io_d1c")
            tk.inc(sS, s.activation(out=scr, in_=iod_s, func=AF.Exp,
                                    scale=zb_s[:, h:h + 1]), f"wmexp{h}")
        SC(_se)
        def _ge(g, h=h, scr=scr):
            tk.wait(g, f"wmexp{h}")
            tk.inc(sG, g.affine_select(out=scr, in_=scr, pattern=[[1, QC]],
                                       compare_op=ALU.is_ge, fill=0.0, base=-1,
                                       channel_multiplier=-1), f"wmmask{h}")
        GP(_ge)
        def _ve(v, h=h, scr=scr):
            tk.wait(v, f"wmmask{h}")
            if h == 0:
                tk.wait(v, "ub")
                tk.wait(v, "identf")
            tk.inc(sV, v.scalar_tensor_tensor(out=wmt_s[:, h, :], in0=identf[:, :],
                                              scalar=ub_s[:, h:h + 1], in1=scr,
                                              op0=ALU.mult, op1=ALU.add), f"wmt{h}")
        VE(_ve)

    def _vz(v):
        tk.wait(v, "zb")
        tk.inc(sV, v.tensor_scalar(out=ub_s[:, :], in0=zb_s[:, :], scalar1=float(QC),
                                   scalar2=None, op0=ALU.mult), "zbq")
    def _vz2(v):
        tk.wait(v, "zb")
        tk.wait(v, f"wmt{H - 1}")
        tk.inc(sV, v.tensor_scalar(out=ub_s[:, :], in0=zb_s[:, :], scalar1=float(QC),
                                   scalar2=None, op0=ALU.mult), "zbq")
    VE(_vz2)
    def _s2(s):
        tk.wait(s, "io_fr")
        tk.wait(s, "zbq")
        for m in range(NE):
            for half, h in ((0, 2 * m), (1, 2 * m + 1)):
                rows = slice(half * 64, half * 64 + 64)
                s.activation(out=wbp_s[rows, m, :], in_=iog_s[rows, 0:QC], func=AF.Exp,
                             scale=zb_s[rows, h:h + 1])
                s.activation(out=wsp_s[rows, m:m + 1], in_=ub_s[rows, h:h + 1],
                             func=AF.Exp)
        tk.inc(sS, s.activation(out=rowb_s, in_=rowb_s, func=AF.Copy),
               "wbp")
    SC(_s2)

    def _g2(g):
        tk.wait(g, "wbp")
        tk.inc(sG, g.iota(out=iog_s, pattern=[[0, 128]], base=QC - 1,
                          channel_multiplier=-1, allow_small_or_imprecise_dtypes=True),
               "io_p127")
    GP(_g2)

    def _s3(s):
        tk.wait(s, "io_p127")
        for h in range(H):
            s.activation(out=wktm_s[:, h * NH:(h + 1) * NH], in_=iog_s[:, 0:NH],
                         func=AF.Exp, scale=zb_s[:, h:h + 1])
        tk.inc(sS, s.activation(out=rowb_s, in_=rowb_s, func=AF.Copy),
               "wkv_tables")
    SC(_s3)

    for name, w_s in (("k", wk_s), ("v", wv_s), ("r", wr_s)):
        mb = {"k": mkb_s, "v": mvb_s, "r": mrb_s}[name]
        omb = {"k": omkb_s, "v": omvb_s, "r": omrb_s}[name]
        for colh in range(2):
            tgn = f"{name}{colh}"
            def _ts(t, w_s=w_s, omb=omb, colh=colh, tgn=tgn):
                if tgn == "k0":
                    tk.wait(t, "omaa")
                else:
                    tk.wait(t, f"s12c_{ptgn}")
                cols = slice(colh * 512, colh * 512 + 512)
                for kc in range(NKC):
                    tk.inc(sT, t.matmul(p_row[0:1, :], omb[:, kc:kc + 1],
                                        w_s[:, kc, cols], start=(kc == 0),
                                        stop=(kc == NKC - 1)),
                           f"s12a_{tgn}" if kc == NKC - 1 else None)
            TE(_ts)
            def _vsa(v, tgn=tgn):
                tk.wait(v, f"s12a_{tgn}")
                tk.inc(sV, v.tensor_copy(out=negm_s[:, :], in_=p_row[0:1, :]),
                       f"s12ca_{tgn}")
            VE(_vsa)
            def _ts2(t, w_s=w_s, mb=mb, colh=colh, tgn=tgn):
                tk.wait(t, f"s12ca_{tgn}")
                cols = slice(colh * 512, colh * 512 + 512)
                for kc in range(NKC):
                    tk.inc(sT, t.matmul(p_row[0:1, :], mb[:, kc:kc + 1],
                                        w_s[:, kc, cols], start=(kc == 0),
                                        stop=(kc == NKC - 1)),
                           f"s12b_{tgn}" if kc == NKC - 1 else None)
            TE(_ts2)
            def _vsb(v, tgn=tgn):
                tk.wait(v, f"s12b_{tgn}")
                tk.inc(sV, v.tensor_copy(out=nm_s[0:1, :], in_=p_row[0:1, :]),
                       f"s12cb_{tgn}")
            VE(_vsb)
            def _ts3(t, tgn=tgn):
                tk.wait(t, f"s12cb_{tgn}")
                t.matmul(p_row[:, :], e01b[:, 0:2], negm_s[:, :], start=True, stop=False)
                tk.inc(sT, t.matmul(p_row[:, :], e01b[:, 2:4], nm_s[0:1, :],
                                    start=False, stop=True), f"s12p_{tgn}")
            TE(_ts3)
            def _vsc(v, colh=colh, name=name, tgn=tgn):
                tk.wait(v, f"s12p_{tgn}")
                cols = slice(colh * 512, colh * 512 + 512)
                tk.inc(sV, v.tensor_copy(out=s12_s[name][:, cols], in_=p_row[:, :]),
                       f"s12c_{tgn}")
            VE(_vsc)
            ptgn = tgn

    def _t2(t):
        tk.wait(t, "s12c_r1")
        for kc in range(NKC):
            tk.inc(sT, t.matmul(p_row[0:1, :], onesb[:, :], wd_s[:, kc, :],
                                start=(kc == 0), stop=(kc == NKC - 1)),
                   "wdsum_mm" if kc == NKC - 1 else None)
        for kc in range(NKC):
            tk.inc(sT, t.matmul(p_sm[0:1, 0:NE], onesb[:, :], wg_s[:, kc, 1:NE + 1],
                                start=(kc == 0), stop=(kc == NKC - 1)),
                   "wgsum_mm" if kc == NKC - 1 else None)
    TE(_t2)

    def _v4(v):
        tk.wait(v, "wdsum_mm")
        tk.inc(sV, v.tensor_copy(out=wdsum_s[:, :], in_=p_row[0:1, :]), "wdsum")
        tk.wait(v, "wgsum_mm")
        tk.inc(sV, v.tensor_copy(out=rowa_s[0:1, 0:NE], in_=p_sm[0:1, 0:NE]),
               "wgsum_row")
    VE(_v4)

    def _t3(t):
        tk.wait(t, "wgsum_row")
        tk.inc(sT, t.matmul(p_sm[:, 32:32 + NE], onesf[:, :], rowa_s[0:1, 0:NE],
                            start=True, stop=True), "wgsb_mm")
    TE(_t3)

    def _v5(v):
        tk.wait(v, "wgsb_mm")
        tk.inc(sV, v.tensor_copy(out=wgsb_s[:, :], in_=p_sm[:, 32:32 + NE]), "wgsb")
    VE(_v5)

    for half in range(2):
        def _tb(t, half=half):
            if half == 0:
                tk.wait(t, "consts")
                tk.wait(t, "bv_row")
            else:
                tk.wait(t, "bvbc0")
            tk.inc(sT, t.matmul(p_bc[:, :], onesbr[:, :],
                                bvb_s[0:1, half * 512:half * 512 + 512],
                                start=True, stop=True), f"bvbm{half}")
        TE(_tb)
        def _vb(v, half=half):
            tk.wait(v, f"bvbm{half}")
            tk.inc(sV, v.tensor_copy(out=bvb_s[:, half * 512:half * 512 + 512],
                                     in_=p_bc[:, :]), f"bvbc{half}")
        VE(_vb)

    for s_, nm_ in ((sT, "IT"), (sV, "IV"), (sS, "IS"), (sG, "IG"), (dX, "IDX"),
                    (dW, "IDW")):
        tk.mark(s_, nm_)
    init_marks = ["IT", "IV", "IS", "IG", "IDX", "IDW"]

    def _bar(e):
        for k in init_marks:
            tk.wait(e, k)
    SY(_bar); SC(_bar); VE(_bar); TE(_bar); GP(_bar)

    def pp_acquire(t, slot):
        tk.wait_opt(t, free["ps4"][slot])

    def ln_stats(i, tag, src, with_gate, src_ev):
        if src is not None:
            def _v(v):
                tk.wait_opt(v, src_ev)
                tk.wait_opt(v, free["xbf"])
                tk.inc(sV, v.tensor_copy(out=xbf_s[:, :, :], in_=src[:, :, :]),
                       f"{tag}_cast:{i}")
            VE(_v)
        def _s(s):
            tk.wait(s, f"{tag}_cast:{i}")
            tk.wait_opt(s, free["xsq"])
            tk.inc(sS, s.activation(out=xsq_s[:, :, :], in_=xbf_s[:, :, :],
                                    func=AF.Square), f"{tag}_sq:{i}")
        SC(_s)
        def _t(t):
            tk.wait(t, f"{tag}_cast:{i}")
            tk.wait_opt(t, free["p_gate"])
            if with_gate:
                for kc in range(NKC):
                    tk.inc(sT, t.matmul(p_gate[:, :], wg_s[:, kc, :], xbf_s[:, kc, :],
                                        start=(kc == 0), stop=(kc == NKC - 1)),
                           f"{tag}_gmm:{i}" if kc == NKC - 1 else None)
            else:
                for kc in range(NKC):
                    tk.inc(sT, t.matmul(p_gate[0:1, :], onesb[:, :], xbf_s[:, kc, :],
                                        start=(kc == 0), stop=(kc == NKC - 1)),
                           f"{tag}_gmm:{i}" if kc == NKC - 1 else None)
            tk.wait(t, f"{tag}_sq:{i}")
            tk.wait_opt(t, free["p_row"])
            for kc in range(NKC):
                tk.inc(sT, t.matmul(p_row[0:1, :], onesb[:, :], xsq_s[:, kc, :],
                                    start=(kc == 0), stop=(kc == NKC - 1)),
                       f"{tag}_ssmm:{i}" if kc == NKC - 1 else None)
        TE(_t)
        def _v2(v):
            srow = p_gate[0:1, :]
            tk.wait(v, f"{tag}_gmm:{i}")
            tk.inc(sV, v.tensor_scalar(out=mrow_s[:, :], in0=srow, scalar1=1.0 / D,
                                       scalar2=None, op0=ALU.mult), f"{tag}_m:{i}")
            tk.wait(v, f"{tag}_ssmm:{i}")
            tk.wait_opt(v, free["trow"])
            v.tensor_tensor(out=trow_s[:, :], in0=mrow_s[:, :], in1=mrow_s[:, :],
                            op=ALU.mult)
            tk.wait_opt(v, free["rsg"])
            v.tensor_scalar(out=rsg_s[:, :], in0=p_row[0:1, :], scalar1=1.0 / D,
                            scalar2=EPS, op0=ALU.mult, op1=ALU.add)
            tk.inc(sV, v.tensor_tensor(out=trow_s[:, :], in0=rsg_s[:, :],
                                       in1=trow_s[:, :], op=ALU.subtract),
                   f"{tag}_var:{i}")
            tk.wait_opt(v, free["negm"])
            tk.inc(sV, v.tensor_scalar(out=negm_s[:, :], in0=mrow_s[:, :], scalar1=-1.0,
                                       scalar2=None, op0=ALU.mult), f"{tag}_negm:{i}")
        VE(_v2)
        free["p_row"] = f"{tag}_var:{i}"
        if not with_gate:
            free["p_gate"] = f"{tag}_m:{i}"
        def _s2(s):
            tk.wait(s, f"{tag}_var:{i}")
            tk.wait_opt(s, free["rsg"])
            s.activation(out=rsg_s[:, :], in_=trow_s[:, :], func=AF.Ln)
            tk.inc(sS, s.activation(out=rsg_s[:, :], in_=rsg_s[:, :], func=AF.Exp,
                                    scale=nhalf_s[0:1, 0:1]),
                   f"{tag}_rsig:{i}")
        SC(_s2)

    def emit_ffn(i, tag, src, dst, src_ev):
        ln_stats(i, tag, src, True, src_ev)
        def _v(v):
            tk.wait(v, f"{tag}_rsig:{i}")
            tk.inc(sV, v.tensor_tensor(out=trow_s[:, :], in0=mrow_s[:, :],
                                       in1=rsg_s[:, :], op=ALU.mult), f"{tag}_mr:{i}")
            tk.wait_opt(v, free["gs"])
            tk.inc(sV, v.tensor_copy(out=gs9_s[:, :], in_=p_gate[:, :]),
                   f"{tag}_gsc:{i}")
        VE(_v)
        free["p_gate"] = f"{tag}_gsc:{i}"
        for m in range(4):
            def _t6(t, m=m):
                tk.wait(t, f"{tag}_negm:{i}")
                pp_acquire(t, m % 4)
                for kc in range(NKC):
                    t.matmul(p_proj[m % 4][:, :], wd_s[:, kc, m * 128:(m + 1) * 128],
                             xbf_s[:, kc, :], start=(kc == 0), stop=False)
                tk.inc(sT, t.matmul(p_proj[m % 4][:, :],
                                    wdsum_s[:, m * 128:(m + 1) * 128], negm_s[:, :],
                                    start=False, stop=True), f"{tag}_hmm:{i}:{m}")
            TE(_t6)
            if m == 0:
                def _t4(t):
                    tk.wait(t, f"{tag}_rsig:{i}")
                    tk.wait_opt(t, free["p_bc"])
                    tk.inc(sT, t.matmul(p_bc[:, :], onesf[:, :], rsg_s[:, :],
                                        start=True, stop=True), f"{tag}_rbm:{i}")
                TE(_t4)
                def _v6(v):
                    tk.wait(v, f"{tag}_rbm:{i}")
                    tk.inc(sV, v.tensor_copy(out=rb_s[:, :], in_=p_bc[:, :]),
                           f"{tag}_rb:{i}")
                VE(_v6)
                free["p_bc"] = f"{tag}_rb:{i}"
                def _t2(t):
                    tk.wait(t, f"{tag}_mr:{i}")
                    tk.wait(t, f"{tag}_gsc:{i}")
                    tk.wait_opt(t, free["p_sm"])
                    for c in range(4):
                        t.transpose(p_sm[:, c * 9:c * 9 + NE + 1],
                                    gs9_s[:, c * 128:(c + 1) * 128],
                                    identf[0:NE + 1, 0:NE + 1])
                        t.matmul(p_sm[:, 64 + 2 * c:64 + 2 * c + 1],
                                 rsg_s[0:1, c * 128:(c + 1) * 128], onesf[0:1, 0:1],
                                 start=True, stop=True)
                        tk.inc(sT, t.matmul(p_sm[:, 65 + 2 * c:65 + 2 * c + 1],
                                            trow_s[0:1, c * 128:(c + 1) * 128],
                                            onesf[0:1, 0:1], start=True, stop=True),
                               f"{tag}_tpm:{i}" if c == 3 else None)
                TE(_t2)
                free["trow"] = f"{tag}_tpm:{i}"
                free["gs"] = f"{tag}_tpm:{i}"
            def _vh(v, m=m):
                tk.wait(v, f"{tag}_hmm:{i}:{m}")
                tk.wait(v, f"{tag}_rb:{i}") if m == 0 else None
                tk.inc(sV, v.tensor_tensor(out=gh_s[:, m, :],
                                           in0=p_proj[m % 4][:, :], in1=rb_s[:, :],
                                           op=ALU.mult), f"{tag}_h1:{i}:{m}")
                v.tensor_scalar(out=gh_s[:, m, :], in0=gh_s[:, m, :],
                                scalar1=bd_s[:, m:m + 1], scalar2=0.0,
                                op0=ALU.add, op1=ALU.max)
            VE(_vh)
            free["ps4"][m % 4] = f"{tag}_h1:{i}:{m}"
        def _v3(v):
            tk.wait(v, f"{tag}_tpm:{i}")
            tk.wait_opt(v, free["gtm"])
            v.tensor_copy(out=lt64_s[:, :], in_=p_sm[:, 0:36])
            tk.inc(sV, v.tensor_copy(out=rst8_s[:, :], in_=p_sm[:, 64:72]),
                   f"{tag}_ltm:{i}")
            ltv = lt64_s.rearrange("p (c x) -> p c x", x=9)[:, :, 1:1 + NE]
            rsv = rst8_s.rearrange("p (c x) -> p c x", x=2)
            v.tensor_tensor(out=gw1_s[:, :, :], in0=bc_last(rsv[:, :, 1:2], NE),
                            in1=bc_mid(wgsb_s[:, :], 4), op=ALU.mult)
            v.tensor_tensor(out=gtm_s[:, :, :], in0=ltv,
                            in1=bc_last(rsv[:, :, 0:1], NE), op=ALU.mult)
            v.tensor_tensor(out=gtm_s[:, :, :], in0=gtm_s[:, :, :], in1=gw1_s[:, :, :],
                            op=ALU.subtract)
            v.reduce_max(out=gm1_s[:, :, :], in_=gtm_s[:, :, :], axis=AX.X)
            v.tensor_tensor(out=gmk1_s[:, :, :], in0=gtm_s[:, :, :],
                            in1=bc_last(gm1_s[:, :, :], NE), op=ALU.is_ge)
            v.scalar_tensor_tensor(out=gw1_s[:, :, :], in0=gmk1_s[:, :, :],
                                   scalar=-1e30, in1=gtm_s[:, :, :], op0=ALU.mult,
                                   op1=ALU.add)
            v.reduce_max(out=gm2_s[:, :, :], in_=gw1_s[:, :, :], axis=AX.X)
            v.tensor_tensor(out=gmk2_s[:, :, :], in0=gw1_s[:, :, :],
                            in1=bc_last(gm2_s[:, :, :], NE), op=ALU.is_ge)
            tk.inc(sV, v.tensor_tensor(out=gde_s[:, :, :], in0=gm2_s[:, :, :],
                                       in1=gm1_s[:, :, :], op=ALU.subtract),
                   f"{tag}_d:{i}")
        VE(_v3)
        free["p_sm"] = f"{tag}_ltm:{i}"
        def _s(s):
            tk.wait(s, f"{tag}_d:{i}")
            tk.inc(sS, s.activation(out=gde_s[:, :, :], in_=gde_s[:, :, :], func=AF.Exp),
                   f"{tag}_e:{i}")
        SC(_s)
        def _v4(v):
            tk.wait(v, f"{tag}_e:{i}")
            v.tensor_scalar(out=gg1_s[:, :, :], in0=gde_s[:, :, :], scalar1=1.0,
                            scalar2=None, op0=ALU.add)
            v.reciprocal(out=gg1_s[:, :, :], in_=gg1_s[:, :, :])
            v.scalar_tensor_tensor(out=gg2_s[:, :, :], in0=gde_s[:, :, :], scalar=0.25,
                                   in1=gg1_s[:, :, :], op0=ALU.mult, op1=ALU.mult)
            v.tensor_scalar(out=gg1_s[:, :, :], in0=gg2_s[:, :, :], scalar1=-1.0,
                            scalar2=0.25, op0=ALU.mult, op1=ALU.add)
            v.tensor_tensor(out=gmk2_s[:, :, :], in0=gmk2_s[:, :, :],
                            in1=bc_last(gg2_s[:, :, :], NE), op=ALU.mult)
            v.tensor_tensor(out=gmk1_s[:, :, :], in0=gmk1_s[:, :, :],
                            in1=bc_last(gg1_s[:, :, :], NE), op=ALU.mult)
            tk.inc(sV, v.tensor_tensor(out=gtm_s[:, :, :], in0=gmk1_s[:, :, :],
                                       in1=gmk2_s[:, :, :], op=ALU.add),
                   f"{tag}_gtm:{i}")
        VE(_v4)
        def _t3(t):
            tk.wait(t, f"{tag}_gtm:{i}")
            tk.wait_opt(t, free["p_bc"])
            for c in range(4):
                tk.inc(sT, t.transpose(p_bc[0:NE, c * 128:(c + 1) * 128],
                                       gtm_s[:, c, :], identf[:, :]),
                       f"{tag}_gfmm:{i}" if c == 3 else None)
        TE(_t3)
        free["gtm"] = f"{tag}_gfmm:{i}"
        def _v5(v):
            tk.wait(v, f"{tag}_gfmm:{i}")
            tk.wait_opt(v, free["gfm"])
            tk.inc(sV, v.tensor_copy(out=gfm_s[:, :], in_=p_bc[0:NE, :]),
                   f"{tag}_gfm:{i}")
        VE(_v5)
        free["p_bc"] = f"{tag}_gfm:{i}"
        for m in range(4):
            def _t5(t, m=m):
                tk.wait(t, f"{tag}_gfm:{i}") if m == 0 else None
                pp_acquire(t, 2 + m % 2)
                tk.inc(sT, t.matmul(p_proj[2 + m % 2][:, :], ind_s[:, m, :],
                                    gfm_s[:, :], start=True, stop=True),
                       f"{tag}_scm:{i}:{m}")
            TE(_t5)
            def _v7(v, m=m):
                tk.wait(v, f"{tag}_scm:{i}:{m}")
                tk.inc(sV, v.tensor_copy(out=scb_s, in_=p_proj[2 + m % 2][:, :]),
                       f"{tag}_scb:{i}:{m}")
                tk.inc(sV, v.tensor_tensor(out=gh_s[:, m, :], in0=gh_s[:, m, :],
                                           in1=scb_s, op=ALU.mult),
                       f"{tag}_hev:{i}:{m}")
            VE(_v7)
            free["ps4"][2 + m % 2] = f"{tag}_scb:{i}:{m}"
        free["xbf"] = f"{tag}_hmm:{i}:3"
        free["xsq"] = f"{tag}_ssmm:{i}"
        free["negm"] = f"{tag}_hmm:{i}:3"
        free["rsg"] = f"{tag}_tpm:{i}"
        for j in range(NKC):
            def _t7(t, j=j):
                tk.wait(t, f"{tag}_hev:{i}:3") if j == 0 else None
                pp_acquire(t, j % 4)
                for kc in range(4):
                    t.matmul(p_proj[j % 4][:, :], wu_s[:, kc, j * 128:(j + 1) * 128],
                             gh_s[:, kc, :], start=(kc == 0), stop=False)
                tk.inc(sT, t.matmul(p_proj[j % 4][:, :], bu_s[:, j * 128:(j + 1) * 128],
                                    gfm_s[:, :], start=False, stop=True),
                       f"{tag}_umm:{i}:{j}")
            TE(_t7)
            def _v8(v, j=j):
                tk.wait(v, f"{tag}_umm:{i}:{j}")
                if tag == "f1":
                    if j == 0 and i > 0:
                        tk.wait(v, f"spx1:{i - 1}")
                    tk.wait(v, f"xr:{i}:{j}")
                    rsrc = xres_s[:, j % 2, :]
                else:
                    rsrc = src[:, j, :]
                tk.inc(sV, v.tensor_tensor(out=dst[:, j, :], in0=p_proj[j % 4][:, :],
                                           in1=rsrc, op=ALU.add),
                       f"{tag}_x1:{i}:{j}")
            VE(_v8)
            free["ps4"][j % 4] = f"{tag}_x1:{i}:{j}"
            if tag == "f1" and j + 2 < NKC:
                def _syx(sync, j=j):
                    tk.wait(sync, f"{tag}_x1:{i}:{j}")
                    tk.inc(dXa if j % 2 == 0 else dXb, sync.dma_start(
                        out=xres_s[:, j % 2, :],
                        in_=x_d.rearrange("(c p) T -> p c T", p=128)[
                            :, j + 2, (i + 1) * TT:(i + 2) * TT]), f"xr:{i}:{j + 2}", 16)
                SY(_syx)
        free["gfm"] = f"{tag}_umm:{i}:{NKC - 1}"
        tk.ev[f"{tag}_done:{i}"] = tk.ev[f"{tag}_x1:{i}:{NKC - 1}"]

    xw = x_d.rearrange("(c p) T -> p c T", p=128)

    def off(i):
        return (i + 1) * TT

    def emit_scan(j):
        for c in range(NW):
            cs = slice(c * QC, (c + 1) * QC)
            gc = j * NW + c
            buf = ktm_s if c % 2 == 0 else ktm2_s
            pgc = gc - 2
            slA, slB = (2 * c) % 4, (2 * c + 1) % 4
            def _tt(t, c=c, cs=cs, j=j, slA=slA, slB=slB):
                if c == 0:
                    tk.wait(t, f"kev:{j}:{NKC - 1}")
                pp_acquire(t, slA)
                pp_acquire(t, slB)
                for m in range(NKC):
                    sl = slA if m < 4 else slB
                    col = (m % 4) * 128
                    tk.inc(sT, t.matmul(p_proj[sl][:, col:col + 128],
                                        kk_s[:, m, cs], identb[:, :],
                                        start=True, stop=True),
                           f"ktp:{j}:{c}" if m == NKC - 1 else None)
            TE(_tt)
            def _vkt(v, c=c, j=j, buf=buf, pgc=pgc, slA=slA, slB=slB):
                tk.wait(v, f"ktp:{j}:{c}")
                if pgc >= 0:
                    tk.wait(v, f"d:{pgc // NW}:{pgc % NW}:{NE - 1}")
                tk.inc(sV, v.tensor_tensor(out=buf[:, 0:512],
                                           in0=p_proj[slA][:, :],
                                           in1=wktm_s[:, 0:512], op=ALU.mult),
                       f"kkwa:{j}:{c}")
                tk.inc(sV, v.tensor_tensor(out=buf[:, 512:1024],
                                           in0=p_proj[slB][:, :],
                                           in1=wktm_s[:, 512:1024], op=ALU.mult),
                       f"kkw:{j}:{c}")
            VE(_vkt)
            free["ps4"][slA] = f"kkwa:{j}:{c}"
            free["ps4"][slB] = f"kkw:{j}:{c}"
            def _syst(sync, gc=gc):
                if gc > 0:
                    tk.wait(sync, f"stu:{gc - 1}:{NE - 1}")
                tk.inc(dSs[gc % 2], sync.dma_start(out=spst_d[gc, :, :],
                                                   in_=st_s[:, gc % 2, :]),
                       f"spst:{gc}", 16)
            SY(_syst)
            pdb = PB[2].ap() if c % 2 == 0 else PB[3].ap()
            for m in range(NE):
                def _td(t, c=c, m=m, j=j, gc=gc, buf=buf, pdb=pdb):
                    if m == 0:
                        tk.wait_opt(t, free["pwd"][c % 2])
                        tk.wait_opt(t, free["p_gate"] if c % 2 == 0
                                    else free["p_row"])
                        tk.wait(t, f"kkw:{j}:{c}")
                        tk.wait(t, f"vev:{j}:{2 * c + 1}")
                    vca = vv_s[:, c, (2 * m) * NH:(2 * m) * NH + NH]
                    vcb = vv_s[:, c, (2 * m + 1) * NH:(2 * m + 1) * NH + NH]
                    t.matmul(pdb[0:64, m * NH:(m + 1) * NH],
                             buf[:, (2 * m) * NH:(2 * m) * NH + NH],
                             vca, start=True, stop=True, tile_position=(0, 0))
                    tk.inc(sT, t.matmul(
                        pdb[64:128, m * NH:(m + 1) * NH],
                        buf[:, (2 * m + 1) * NH:(2 * m + 1) * NH + NH],
                        vcb, start=True, stop=True, tile_position=(0, 64)),
                        f"d:{j}:{c}:{m}")
                TE(_td)
            def _vst(v, c=c, j=j, gc=gc, pdb=pdb):
                tk.wait(v, f"d:{j}:{c}:{NE - 1}")
                if gc >= 1:
                    tk.wait(v, f"spst:{gc - 1}")
                wsb = bass.AP(wsp_s[:, :].tensor, wsp_s[:, :].offset,
                              list(wsp_s[:, :].ap) + [[0, NH]])
                stn = st_s[:, (gc + 1) % 2, :]
                v.tensor_tensor(
                    out=stn.rearrange("p (m x) -> p m x", m=NE),
                    in0=st_s[:, gc % 2, :].rearrange("p (m x) -> p m x", m=NE),
                    in1=wsb, op=ALU.mult)
                tk.inc(sV, v.tensor_tensor(out=stn, in0=stn, in1=pdb[:, :],
                                           op=ALU.add), f"stu:{gc}:{NE - 1}")
            VE(_vst)
            free["pwd"][c % 2] = f"stu:{gc}:{NE - 1}"
        free["p_gate"] = f"stu:{j * NW + 2}:{NE - 1}"
        free["p_row"] = f"stu:{j * NW + 3}:{NE - 1}"

    for i in range(-1, n_tiles):
        pre = (i == -1)
        emit_ffn(i, "f1", None, x1_s, None)
        if i + 1 < n_tiles:
            def _sy2(sync, i=i):
                for j, dsem in ((0, dXa), (1, dXb)):
                    tk.wait(sync, f"f1_x1:{i}:{6 + j}")
                    tk.inc(dsem, sync.dma_start(
                        out=xres_s[:, j, :],
                        in_=xw[:, j, off(i + 1):off(i + 1) + TT]), f"xr:{i + 1}:{j}", 16)
            SY(_sy2)
        if not pre:
            def _syx1(sync, i=i):
                tk.wait(sync, f"f1_done:{i}")
                tk.inc(dSx, sync.dma_start(
                    out=spx1_d.rearrange("(c p) T -> p c T", p=128)[
                        :, :, i * TT:(i + 1) * TT],
                    in_=x1_s[:, :, :]), f"spx1:{i}", 16)
            SY(_syx1)
        ln_stats(i, "a", x1_s, False, f"f1_done:{i}")
        def _va(v, i=i):
            tk.wait(v, f"a_rsig:{i}")
            tk.inc(sV, v.tensor_tensor(out=trow_s[:, :], in0=mrow_s[:, :],
                                       in1=rsg_s[:, :], op=ALU.mult), f"a_mr:{i}")
            tk.wait_opt(v, free["ngmr"])
            if i > -1:
                v.tensor_copy(out=ngmr_s[:, 0:1], in_=ngmr_s[:, TT:TT + 1])
            if i == -1:
                v.tensor_scalar(out=ngmr_s[:, 1:TT + 1], in0=trow_s[:, :],
                                scalar1=-1.0, scalar2=None, op0=ALU.mult)
                tk.wait(v, "flag")
                tk.inc(sV, v.tensor_scalar(out=ngmr_s[:, TT:TT + 1],
                                           in0=ngmr_s[:, TT:TT + 1],
                                           scalar1=flag_s[0:1, 0:1], scalar2=None,
                                           op0=ALU.mult), f"a_ngmr:{i}")
            else:
                tk.inc(sV, v.tensor_scalar(out=ngmr_s[:, 1:TT + 1], in0=trow_s[:, :],
                                           scalar1=-1.0, scalar2=None, op0=ALU.mult),
                       f"a_ngmr:{i}")
        VE(_va)
        free["ngmr"] = f"a_spnm:{i}"
        def _ta(t, i=i, pre=pre):
            if not pre:
                tk.wait(t, f"a_ngmr:{i}")
                tk.wait_opt(t, free["p_row"])
                t.matmul(p_row[:, :], e01b[:, 0:2], ngmr_s[:, 1:TT + 1], start=True,
                         stop=False)
                tk.inc(sT, t.matmul(p_row[:, :], e01b[:, 2:4], ngmr_s[:, 0:TT],
                                    start=False, stop=True), f"a_nmm:{i}")
            if pre:
                tk.wait(t, f"a_rsig:{i}")
            tk.wait_opt(t, free["p_bc"])
            tk.inc(sT, t.matmul(p_bc[:, :], onesf[:, :], rsg_s[:, :], start=True,
                                stop=True), f"a_rbm:{i}")
        TE(_ta)
        free["rsg"] = f"a_rbm:{i}"
        def _va2(v, i=i, pre=pre):
            if not pre:
                tk.wait(v, f"a_nmm:{i}")
                tk.wait_opt(v, free["negm"])
                tk.inc(sV, v.tensor_copy(out=nm_s[:, :], in_=p_row[:, :]), f"a_nm:{i}")
            tk.wait(v, f"a_rbm:{i}")
            tk.inc(sV, v.tensor_copy(out=rb_s[:, :], in_=p_bc[:, :]), f"a_rb:{i}")
            if i > -1:
                tk.wait(v, f"a_spnm:{i - 1}")
                v.tensor_copy(out=t1_s[:, :, 0:1], in_=t1_s[:, :, TT:TT + 1])
            if pre:
                v.tensor_tensor(out=t1_s[:, :, 1:TT + 1], in0=x1_s[:, :, :],
                                in1=bc_mid(rb_s[:, :], NKC), op=ALU.mult)
                tk.inc(sV, v.tensor_scalar(out=t1_s[:, :, TT:TT + 1],
                                           in0=t1_s[:, :, TT:TT + 1],
                                           scalar1=flag_s[:, 0:1], scalar2=None,
                                           op0=ALU.mult), f"a_t1:{i}")
            else:
                tk.inc(sV, v.tensor_tensor(out=t1_s[:, :, 1:TT + 1], in0=x1_s[:, :, :],
                                           in1=bc_mid(rb_s[:, :], NKC), op=ALU.mult),
                       f"a_t1:{i}")
            if not pre:
                tk.wait(v, f"a_gmm:{i}")
                tk.inc(sV, v.tensor_tensor(out=xbf_s[:, :, :], in0=t1_s[:, :, 0:TT],
                                           in1=t1_s[:, :, 1:TT + 1], op=ALU.subtract),
                       f"a_diff:{i}")
        VE(_va2)
        if not pre:
            free["p_row"] = f"a_nm:{i}"
        free["p_bc"] = f"a_rb:{i}"
        free["xbf"] = f"a_diff:{i}" if not pre else free["xbf"]
        def _syt1(sync, i=i, pre=pre):
            tk.wait(sync, f"a_t1:{i}")
            if pre:
                tk.inc(dSt, sync.dma_start(
                    out=spt1_d.rearrange("(c p) T -> p c T", p=128)[:, :, 0:1],
                    in_=t1_s[:, :, TT:TT + 1],
                    allow_slow_non_contiguous=True), f"spt1:{i}", 16)
                tk.wait(sync, f"a_ngmr:{i}")
                tk.inc(dSt, sync.dma_start(out=spnm_d[:, 0:1],
                                          in_=ngmr_s[:, TT:TT + 1]), f"a_spnm:{i}", 16)
            else:
                tk.inc(dSt, sync.dma_start(
                    out=spt1_d.rearrange("(c p) T -> p c T", p=128)[
                        :, :, 1 + i * TT:1 + (i + 1) * TT],
                    in_=t1_s[:, :, 1:TT + 1]), f"spt1:{i}", 16)
                tk.wait(sync, f"a_ngmr:{i}")
                tk.inc(dSt, sync.dma_start(out=spnm_d[:, 1 + i * TT:1 + (i + 1) * TT],
                                          in_=ngmr_s[:, 1:TT + 1]), f"a_spnm:{i}", 16)
        SY(_syt1)
        if pre:
            tk.ev[f"a_spnm_pre"] = tk.ev[f"a_spnm:{i}"]
            def _gx0(g, i=i):
                tk.wait(g, f"a_diff_pre_guard") if False else None
                tk.wait(g, f"a_ssmm:{i}")
                tk.wait(g, f"a_gmm:{i}")
                tk.inc(dW, g.dma_start(
                    out=xbf_s[:, :, :],
                    in_=xw[:, :, off(i + 1):off(i + 1) + TT]), f"f1_cast:{i + 1}", 16)
            GP(_gx0)
            continue

        def mix(name, maa, dst_b, extra_ev, i=i):
            def _vm(v):
                tk.wait(v, f"a_diff:{i}")
                tk.wait_opt(v, extra_ev)
                for kc in range(NKC):
                    tk.inc(sV, v.scalar_tensor_tensor(
                        out=dst_b[:, kc, :], in0=xbf_s[:, kc, :],
                        scalar=maa[:, kc:kc + 1],
                        in1=t1_s[:, kc, 1:TT + 1], op0=ALU.mult, op1=ALU.add),
                        f"a_mx{name}:{i}" if kc == NKC - 1 else None)
            VE(_vm)

        mix("k", mkb_s[:, :], xsq_s, f"a_ssmm:{i}")
        free["xsq"] = None
        mix("v", mvb_s[:, :], xv_s, None)
        if i + 1 < n_tiles:
            def _gx1(g, i=i):
                tk.wait(g, f"a_mxv:{i}")
                tk.inc(dW, g.dma_start(
                    out=xbf_s[:, :, :],
                    in_=xw[:, :, off(i + 1):off(i + 1) + TT]), f"f1_cast:{i + 1}", 16)
            GP(_gx1)
        if i >= 1:
            emit_scan(i - 1)
        for m in range(NKC):
            def _tk2(t, m=m, i=i):
                tk.wait(t, f"a_mxk:{i}") if m == 0 else None
                tk.wait(t, f"a_nm:{i}") if m == 0 else None
                pp_acquire(t, m % 4)
                for kc in range(NKC):
                    t.matmul(p_proj[m % 4][:, :], wk_s[:, kc, m * 128:(m + 1) * 128],
                             xsq_s[:, kc, :], start=(kc == 0), stop=False)
                t.matmul(p_proj[m % 4][:, :],
                         s12_s["k"][:, m * 128:(m + 1) * 128], nm_s[:, :],
                         start=False, stop=False)
                tk.inc(sT, t.matmul(p_proj[m % 4][:, :],
                                    bkr_s[:, m * 128:(m + 1) * 128], onestt_s[:, :],
                                    start=False, stop=True), f"kmm:{i}:{m}")
            TE(_tk2)
            def _sk(s, m=m, i=i):
                if m == 0:
                    if i > 0:
                        tk.wait(s, f"ktp:{i - 1}:{NW - 1}")
                        tk.wait(s, f"spkk:{i - 1}")
                tk.wait(s, f"kmm:{i}:{m}")
                tk.inc(sS, s.activation(out=kk_s[:, m, :], in_=p_proj[m % 4][:, :],
                                        func=AF.Copy), f"kev:{i}:{m}")
            SC(_sk)
            free["ps4"][m % 4] = f"kev:{i}:{m}"
        def _sykk(sync, i=i):
            tk.wait(sync, f"kev:{i}:{NKC - 1}")
            tk.inc(dSk, sync.dma_start(
                out=spkk_d.rearrange("(c p) T -> p c T", p=128)[
                    :, :, i * TT:(i + 1) * TT],
                in_=kk_s[:, :, :]), f"spkk:{i}", 16)
        SY(_sykk)

        for idx in range(NKC):
            sub, half = idx // 2, idx % 2
            def _tv(t, idx=idx, sub=sub, half=half, i=i):
                tk.wait(t, f"a_mxv:{i}") if idx == 0 else None
                pp_acquire(t, idx % 4)
                for kc in range(NKC):
                    t.matmul(p_proj[idx % 4][:, :],
                             xv_s[:, kc, sub * 128:(sub + 1) * 128],
                             wv_s[:, kc, half * 512:half * 512 + 512],
                             start=(kc == 0), stop=False)
                tk.inc(sT, t.matmul(p_proj[idx % 4][:, :],
                                    nm_s[:, sub * 128:(sub + 1) * 128],
                                    s12_s["v"][:, half * 512:half * 512 + 512],
                                    start=False, stop=True), f"vmm:{i}:{idx}")
            TE(_tv)
            def _vv2(v, idx=idx, sub=sub, half=half, i=i):
                tk.wait(v, f"vmm:{i}:{idx}")
                if idx == 0 and i > 0:
                    tk.wait(v, f"spvv:{i - 1}")
                tk.inc(sV, v.tensor_tensor(
                    out=vv_s[:, sub, half * 512:half * 512 + 512],
                    in0=p_proj[idx % 4][:, :],
                    in1=bvb_s[:, half * 512:half * 512 + 512],
                    op=ALU.add), f"vev:{i}:{idx}")
            VE(_vv2)
            free["ps4"][idx % 4] = f"vev:{i}:{idx}"
        def _syvv(sync, i=i):
            tk.wait(sync, f"vev:{i}:{NKC - 1}")
            tk.inc(dSv, sync.dma_start(
                out=spvv_d.rearrange("(ti x) d -> ti x d", x=TT)[i, :, :]
                .rearrange("(s p) d -> p s d", p=128),
                in_=vv_s[:, :, :]), f"spvv:{i}", 16)
        SY(_syvv)

        free["xsq"] = f"kmm:{i}:{NKC - 1}"
        free["negm"] = f"vmm:{i}:{NKC - 1}"

    emit_scan(n_tiles - 1)

    n_gc = n_tiles * NW
    free["p_gate"] = f"stu:{n_gc - 2}:{NE - 1}"
    free["p_row"] = f"stu:{n_gc - 1}:{NE - 1}"
    def _gcc(g):
        tk.wait(g, f"stu:{n_gc - 1}:{NE - 1}")
        tk.inc(dC, g.dma_start(out=ccin_d.ap(), in_=st_s[:, n_gc % 2, :]), step=16)
        g.wait_ge(dC, 16)
        g.collective_compute(
            "AllGather", ALU.bypass,
            replica_groups=[[0, 1], [2, 3], [4, 5], [6, 7]],
            ins=[ccin_d.ap().opt()], outs=[ccout_d.ap().opt()]).then_inc(sG, 1)
        tk.cnt[sG.num] = tk.cnt.get(sG.num, 0) + 1
        tk.mark(sG, "ccdone")
        tk.wait(g, "ccdone")
        tk.wait(g, f"f1_x1:{n_tiles - 1}:{NKC - 1}")
        tk.inc(dC, g.dma_start(out=sigma_s, in_=ccout_d[0, :, :]), "sigld", 16)
    GP(_gcc)
    def _vcc(v):
        tk.wait(v, "sigld")
        tk.wait(v, "flag")
        tk.inc(sV, v.tensor_scalar(out=sigd_s, in0=sigma_s, scalar1=flag_s[:, 0:1],
                                   scalar2=None, op0=ALU.mult), "sigf")
    VE(_vcc)

    for i in range(n_tiles):
        def _syl(sync, i=i):
            if i == 0:
                tk.wait(sync, f"a_spnm:{n_tiles - 1}")
            else:
                tk.wait(sync, f"b_mxr:{i - 1}")
            tk.inc(dLt, sync.dma_start(
                out=t1_s[:, :, 0:TT + 1],
                in_=spt1_d.rearrange("(c p) T -> p c T", p=128)[
                    :, :, i * TT:i * TT + TT + 1]), f"ldt1:{i}", 16)
            if i == 0:
                tk.wait(sync, f"a_spnm:{n_tiles - 1}")
            else:
                tk.wait(sync, f"b_nmm:{i - 1}")
            tk.inc(dLn, sync.dma_start(
                out=ngmr_s[:, 0:TT + 1],
                in_=spnm_d[:, i * TT:i * TT + TT + 1]), f"ldnm:{i}", 16)
            if i == 0:
                tk.wait(sync, f"spkk:{n_tiles - 1}")
                tk.wait(sync, f"ktp:{n_tiles - 1}:{NW - 1}")
            else:
                tk.wait(sync, f"at:{i - 1}:{NW - 1}:{NE - 1}")
            tk.inc(dLk, sync.dma_start(
                out=kk_s[:, :, :],
                in_=spkk_d.rearrange("(c p) T -> p c T", p=128)[
                    :, :, i * TT:(i + 1) * TT]), f"ldkk:{i}", 16)
            if i == 0:
                tk.wait(sync, f"spvv:{n_tiles - 1}")
                tk.wait(sync, f"d:{n_tiles - 1}:{NW - 1}:{NE - 1}")
            else:
                tk.wait(sync, f"y:{i - 1}:{NW - 1}:{NE - 1}")
            tk.inc(dLv, sync.dma_start(
                out=vv_s[:, :, :],
                in_=spvv_d.rearrange("(ti x) d -> ti x d", x=TT)[i, :, :]
                .rearrange("(s p) d -> p s d", p=128)), f"ldvv:{i}", 16)
            if i == 0:
                tk.wait(sync, f"a_t1:{n_tiles - 1}")
                tk.wait(sync, f"spx1:{n_tiles - 1}")
            else:
                tk.wait(sync, f"out:{i - 1}")
            tk.inc(dLx, sync.dma_start(
                out=x1_s[:, :, :],
                in_=spx1_d.rearrange("(c p) T -> p c T", p=128)[
                    :, :, i * TT:(i + 1) * TT]), f"ldx1:{i}", 16)
        SY(_syl)
        def _tb(t, i=i):
            tk.wait(t, f"ldnm:{i}")
            tk.wait_opt(t, free["p_row"])
            t.matmul(p_row[:, :], e01b[:, 0:2], ngmr_s[:, 1:TT + 1], start=True,
                     stop=False)
            tk.inc(sT, t.matmul(p_row[:, :], e01b[:, 2:4], ngmr_s[:, 0:TT],
                                start=False, stop=True), f"b_nmm:{i}")
        TE(_tb)
        def _vb(v, i=i):
            tk.wait(v, f"b_nmm:{i}")
            tk.wait_opt(v, free["negm"])
            tk.inc(sV, v.tensor_copy(out=nm_s[:, :], in_=p_row[:, :]), f"b_nm:{i}")
            tk.wait(v, f"ldt1:{i}")
            tk.wait_opt(v, free["xbf"])
            tk.inc(sV, v.tensor_tensor(out=xbf_s[:, :, :], in0=t1_s[:, :, 0:TT],
                                       in1=t1_s[:, :, 1:TT + 1], op=ALU.subtract),
                   f"b_diff:{i}")
            tk.wait_opt(v, free["xsq"])
            for kc in range(NKC):
                tk.inc(sV, v.scalar_tensor_tensor(
                    out=xsq_s[:, kc, :], in0=xbf_s[:, kc, :],
                    scalar=mrb_s[:, kc:kc + 1],
                    in1=t1_s[:, kc, 1:TT + 1], op0=ALU.mult, op1=ALU.add),
                    f"b_mxr:{i}" if kc == NKC - 1 else None)
        VE(_vb)
        free["p_row"] = f"b_nm:{i}"
        free["xbf"] = f"b_diff:{i}"
        for m in range(NKC):
            def _tr(t, m=m, i=i):
                tk.wait(t, f"b_mxr:{i}") if m == 0 else None
                pp_acquire(t, m % 4)
                for kc in range(NKC):
                    t.matmul(p_proj[m % 4][:, :], wr_s[:, kc, m * 128:(m + 1) * 128],
                             xsq_s[:, kc, :], start=(kc == 0), stop=False)
                t.matmul(p_proj[m % 4][:, :],
                         s12_s["r"][:, m * 128:(m + 1) * 128], nm_s[:, :],
                         start=False, stop=False)
                tk.inc(sT, t.matmul(p_proj[m % 4][:, :],
                                    brr_s[:, m * 128:(m + 1) * 128], onestt_s[:, :],
                                    start=False, stop=True), f"rmm:{i}:{m}")
            TE(_tr)
            def _sr(s, m=m, i=i):
                tk.wait(s, f"rmm:{i}:{m}")
                tk.inc(sS, s.activation(out=rr_s[:, m, :], in_=p_proj[m % 4][:, :],
                                        func=AF.Copy), f"rev:{i}:{m}")
            SC(_sr)
            free["ps4"][m % 4] = f"rev:{i}:{m}"
        free["xsq"] = f"rmm:{i}:{NKC - 1}"
        free["negm"] = f"rmm:{i}:{NKC - 1}"

        for c in range(NW):
            cs = slice(c * QC, (c + 1) * QC)
            gc = i * NW + c
            def _syst2(sync, gc=gc):
                if gc == 0:
                    tk.wait(sync, "sigld")
                if gc < 2:
                    lastp = n_gc - 1 if (n_gc - 1) % 2 == gc % 2 else n_gc - 2
                    tk.wait(sync, f"spst:{lastp}")
                if gc >= 2:
                    tk.wait(sync, f"y:{(gc - 2) // NW}:{(gc - 2) % NW}:{NE - 1}")
                tk.inc(dLs[gc % 2], sync.dma_start(out=st_s[:, gc % 2, :],
                                                   in_=spst_d[gc, :, :]),
                       f"ldst:{gc}", 16)
            SY(_syst2)
            def _vsc(v, gc=gc, i=i, c=c):
                tk.wait(v, f"ldst:{gc}")
                tk.wait(v, "sigf") if gc == 0 else None
                if gc >= 2:
                    tk.wait(v, f"y:{(gc - 2) // NW}:{(gc - 2) % NW}:{NE - 1}")
                tk.inc(sV, v.tensor_tensor(out=stb_s[:, gc % 2, :],
                                           in0=st_s[:, gc % 2, :], in1=sigd_s,
                                           op=ALU.add), f"stc:{gc}")
                wspb = bc_last(wsp_s[:, :, None] if False else
                               bass.AP(wsp_s[:, :].tensor, wsp_s[:, :].offset,
                                       list(wsp_s[:, :].ap) + [[0, NH]]), NH) \
                    if False else bass.AP(wsp_s[:, :].tensor, wsp_s[:, :].offset,
                                          list(wsp_s[:, :].ap) + [[0, NH]])
                tk.inc(sV, v.tensor_tensor(out=sigd_s, in0=sigd_s, in1=wspb,
                                           op=ALU.mult), f"sigu:{gc}")
            VE(_vsc)
            def _grr(g, c=c, cs=cs, i=i):
                tk.wait(g, f"rev:{i}:{NKC - 1}") if c == 0 else None
                if i > 0 or c > 0:
                    pc = (i, c - 1) if c > 0 else (i - 1, NW - 1)
                    tk.wait(g, "y:%d:%d:%d" % (pc[0], pc[1], NE - 1))
                tk.inc(sG, g.tensor_tensor(out=rrw_s[:, :, :], in0=rr_s[:, :, cs],
                                           in1=wbp_s[:, :, :], op=ALU.mult),
                       f"rrw:{i}:{c}")
            GP(_grr)
            def mk_tat(m, c=c, cs=cs, i=i):
                sl = m % 2
                atA = p_proj[2 + sl][:, 0:QC]
                atB = p_proj[sl][:, 0:QC]
                def _tat(t, c=c, cs=cs, m=m, atA=atA, atB=atB, i=i):
                    if c == 0 and m < 2:
                        tk.wait(t, f"rev:{i}:{NKC - 1}")
                        tk.wait(t, f"ldkk:{i}")
                    pp_acquire(t, 2 + m % 2)
                    pp_acquire(t, m % 2)
                    t.matmul(atA, kk_s[0:64, m, cs], rr_s[0:64, m, cs],
                             start=True, stop=True, tile_position=(0, 0))
                    tk.inc(sT, t.matmul(atB, kk_s[64:128, m, cs],
                                        rr_s[64:128, m, cs], start=True, stop=True,
                                        tile_position=(64, 0)), f"at:{i}:{c}:{m}")
                def _vw(v, c=c, m=m, atA=atA, atB=atB, i=i):
                    tk.wait(v, f"at:{i}:{c}:{m}")
                    v.tensor_tensor(out=atw_s[:, m % 2, 0:QC], in0=atA,
                                    in1=wmt_s[:, 2 * m, :], op=ALU.mult)
                    tk.inc(sV, v.tensor_tensor(out=atw_s[:, m % 2, QC:2 * QC],
                                               in0=atB, in1=wmt_s[:, 2 * m + 1, :],
                                               op=ALU.mult), f"atw:{i}:{c}:{m}")
                TE(_tat); VE(_vw)
                free["ps4"][2 + m % 2] = f"atw:{i}:{c}:{m}"
                free["ps4"][m % 2] = f"atw:{i}:{c}:{m}"
            mk_tat(0)
            mk_tat(1)
            for m in range(NE):
                sl = m % 2
                y_sl = py_sl[sl]
                def _ty(t, c=c, m=m, y_sl=y_sl, i=i, gc=gc):
                    tk.wait(t, f"atw:{i}:{c}:{m}")
                    tk.wait_opt(t, free["pwy"][m % 2])
                    if c == 0 and m < 2:
                        tk.wait_opt(t, free["p_bc"])
                        tk.wait_opt(t, free["p_sm"])
                    tk.wait(t, f"rrw:{i}:{c}") if m == 0 else None
                    tk.wait(t, f"ldvv:{i}") if m == 0 else None
                    tk.wait(t, f"stc:{gc}") if m == 0 else None
                    vca = vv_s[:, c, (2 * m) * NH:(2 * m) * NH + NH]
                    vcb = vv_s[:, c, (2 * m + 1) * NH:(2 * m + 1) * NH + NH]
                    stbv = stb_s[:, gc % 2, :]
                    t.matmul(y_sl[0:64, :], vca, atw_s[:, m % 2, 0:QC], start=True,
                             stop=False, tile_position=(0, 0))
                    t.matmul(y_sl[0:64, :], stbv[0:64, m * NH:(m + 1) * NH],
                             rrw_s[0:64, m, :], start=False, stop=True,
                             tile_position=(0, 0))
                    t.matmul(y_sl[64:128, :], vcb, atw_s[:, m % 2, QC:2 * QC],
                             start=True, stop=False, tile_position=(0, 64))
                    tk.inc(sT, t.matmul(y_sl[64:128, :],
                                        stbv[64:128, m * NH:(m + 1) * NH],
                                        rrw_s[64:128, m, :], start=False, stop=True,
                                        tile_position=(64, 64)), f"y:{i}:{c}:{m}")
                def _sw(s, c=c, cs=cs, m=m, y_sl=y_sl, i=i):
                    tk.wait(s, f"y:{i}:{c}:{m}")
                    tk.inc(sS, s.activation(out=rr_s[:, m, cs], in_=y_sl,
                                            func=AF.Copy), f"yev:{i}:{c}:{m}")
                TE(_ty); SC(_sw)
                free["pwy"][m % 2] = f"yev:{i}:{c}:{m}"
                if m + 2 < NE:
                    mk_tat(m + 2)

        for m in range(NKC):
            def _to(t, m=m, i=i):
                tk.wait(t, f"yev:{i}:{NW - 1}:{NE - 1}") if m == 0 else None
                tk.wait(t, f"ldx1:{i}") if m == 0 else None
                pp_acquire(t, m % 4)
                for kc in range(NKC):
                    tk.inc(sT, t.matmul(p_proj[m % 4][:, :],
                                        wo_s[:, kc, m * 128:(m + 1) * 128],
                                        rr_s[:, kc, :], start=(kc == 0),
                                        stop=(kc == NKC - 1)),
                           f"omm:{i}:{m}" if kc == NKC - 1 else None)
            TE(_to)
            def _vo(v, m=m, i=i):
                tk.wait(v, f"omm:{i}:{m}")
                tk.inc(sV, v.scalar_tensor_tensor(out=x1_s[:, m, :],
                                                  in0=p_proj[m % 4][:, :],
                                                  scalar=bo_s[:, m:m + 1],
                                                  in1=x1_s[:, m, :], op0=ALU.add,
                                                  op1=ALU.add), f"x2:{i}:{m}")
            VE(_vo)
            free["ps4"][m % 4] = f"x2:{i}:{m}"
        tk.ev[f"x2d:{i}"] = tk.ev[f"x2:{i}:{NKC - 1}"]
        emit_ffn(i, "f3", x1_s, x1_s, f"x2d:{i}")
        def _go(g, i=i):
            tk.wait(g, f"f3_done:{i}")
            tk.inc(dO, g.dma_start(
                out=out_d.rearrange("(c p) T -> p c T", p=128)[
                    :, :, i * TT:(i + 1) * TT],
                in_=x1_s[:, :, :]), f"out:{i}", 16)
        GP(_go)

    def _fin(g):
        g.wait_ge(dO, tk.cnt[dO.num])
    GP(_fin)

    cnt1 = dict(tk.cnt)
    tk.pass2 = True
    tk.cnt = {}
    with nc.Block() as blk:
        for key, attr in (("y", "sync"), ("s", "scalar"), ("v", "vector"),
                          ("t", "tensor"), ("g", "gpsimd")):
            def run_all(e, key=key):
                if key == "s":
                    wrap = SEng(e, sS, tk)
                elif key == "v":
                    wrap = SEng(e, sV, tk)
                elif key == "g":
                    wrap = SEng(e, sG, tk)
                else:
                    wrap = e
                for k2, idx, f in EMITS:
                    if k2 == key:
                        tk.cur = idx
                        f(wrap)
            getattr(blk, attr)(run_all)
    for k, v in cnt1.items():
        assert tk.cnt.get(k, 0) == v, (k, v, tk.cnt.get(k, 0))

    ctx.close()
    return nc


def _prep_inputs(inputs):
    f = np.ascontiguousarray
    x = np.asarray(inputs["L0_attn_x"], np.float32)
    B, T, _ = x.shape
    base = {
        "WrT": f(np.asarray(inputs["L0_attn_Wr"], np.float32).T),
        "WkT": f(np.asarray(inputs["L0_attn_Wk"], np.float32).T),
        "WvT": f(np.asarray(inputs["L0_attn_Wv"], np.float32).T),
        "WoT": f(np.asarray(inputs["L0_attn_Wo"], np.float32).T),
        "WdT": f(np.asarray(inputs["L0_ffn_Wd"], np.float32).transpose(1, 0, 2)
                 .reshape(D, HD)),
        "WuT": f(np.asarray(inputs["L0_ffn_Wu"], np.float32).reshape(HD, D)),
        "wgate": f(np.asarray(inputs["L0_ffn_w_gate"], np.float32)),
        "bu": f(np.asarray(inputs["L0_ffn_bu"], np.float32)),
        "bd": f(np.asarray(inputs["L0_ffn_bd"], np.float32).reshape(HD)),
        "br": f(np.asarray(inputs["L0_attn_br"], np.float32)),
        "bk": f(np.asarray(inputs["L0_attn_bk"], np.float32)),
        "bv": f(np.asarray(inputs["L0_attn_bv"], np.float32)),
        "bo": f(np.asarray(inputs["L0_attn_bo"], np.float32)),
        "maak": f(np.asarray(inputs["L0_attn_time_maa_k"], np.float32)),
        "maav": f(np.asarray(inputs["L0_attn_time_maa_v"], np.float32)),
        "maar": f(np.asarray(inputs["L0_attn_time_maa_r"], np.float32)),
        "tdecay": f(np.asarray(inputs["L0_attn_time_decay"], np.float32)),
        "tfaaaa": f(np.asarray(inputs["L0_attn_time_faaaa"], np.float32)),
    }
    return base, x, B, T


def _core_inputs(base, x, B, T, core):
    half_T = T // 2
    b = (core // 2) % B
    half = core % 2
    lo = half * half_T
    xb = x[b].T
    win = np.zeros((D, TT + half_T), np.float32)
    if half == 1:
        win[:, 0:TT] = xb[:, lo - TT:lo]
    win[:, TT:] = xb[:, lo:lo + half_T]
    m = dict(base)
    m["x"] = np.ascontiguousarray(win)
    m["flag"] = np.full((128, 1), float(half), np.float32)
    return m


def kernel(**inputs):
    base, x, B, T = _prep_inputs(inputs)
    half_T = T // 2
    nc = build_nc(half_T)
    in_maps = [_core_inputs(base, x, B, T, c) for c in range(8)]
    res = run_bass_kernel_spmd(nc, in_maps, core_ids=list(range(8)))
    out = np.empty((B, T, D), np.float32)
    for b in range(B):
        for half in range(2):
            c = b * 2 + half
            out[b, half * half_T:(half + 1) * half_T] = res.results[c]["out"].T
    return out


# version 15 (final)
# speedup vs baseline: 1.0503x; 1.0503x over previous
import contextlib
import numpy as np
import concourse.bass as bass
import concourse.mybir as mybir
from concourse.bass_utils import run_bass_kernel_spmd

F32 = mybir.dt.float32
BF16 = mybir.dt.bfloat16
AF = mybir.ActivationFunctionType
ALU = mybir.AluOpType
AX = mybir.AxisListType

D = 1024
NKC = 8
H = 16
NH = 64
QC = 128
TT = 512
NW = TT // QC
NE = 8
HD = 512
EPS = 1e-5


def bc_last(ap, n):
    return bass.AP(ap.tensor, ap.offset, list(ap.ap[:-1]) + [[0, n]])


def bc_mid(ap, n):
    return bass.AP(ap.tensor, ap.offset, [ap.ap[0], [0, n]] + list(ap.ap[1:]))


class Trk:

    def __init__(self):
        self.cnt = {}
        self.ev = {}
        self.auto = set()
        self.pass2 = False
        self.wlog = {}
        self.cur = None

    def inc(self, sem, instr, key=None, step=1):
        if sem.num in self.auto:
            if key is not None:
                self.ev[key] = (sem, self.cnt.get(sem.num, 0))
            return instr
        c = self.cnt.get(sem.num, 0) + step
        self.cnt[sem.num] = c
        instr.then_inc(sem, step)
        if key is not None:
            self.ev[key] = (sem, c)
        return instr

    def mark(self, sem, key):
        self.ev[key] = (sem, self.cnt.get(sem.num, 0))

    def wait(self, eng, key):
        if self.pass2:
            w = self.wlog[self.cur].pop(0)
            if w is not None:
                eng.wait_ge(w[0], w[1])
            return
        sem, v = self.ev[key]
        self.wlog.setdefault(self.cur, []).append((sem, v))
        eng.wait_ge(sem, v)

    def wait_opt(self, eng, key):
        if self.pass2:
            self.wait(eng, None)
            return
        if key is not None:
            self.wait(eng, key)
        else:
            self.wlog.setdefault(self.cur, []).append(None)


class MockInstr:
    def then_inc(self, sem, step=1):
        return self


class MockEng:
    def wait_ge(self, sem, val):
        return MockInstr()

    def __getattr__(self, nm):
        def f(*a, **k):
            return MockInstr()
        return f


_SENG_EXEMPT = {"wait_ge", "dma_start", "print", "then_inc", "engine",
                "collective_compute", "breakpoint"}


class SEng:

    def __init__(self, eng, sem, tk):
        self._eng = eng
        self._sem = sem
        self._tk = tk

    def __getattr__(self, nm):
        a = getattr(self._eng, nm)
        if nm.startswith("_") or nm in _SENG_EXEMPT or not callable(a):
            return a

        def wrap(*args, **kw):
            c = self._tk.cnt.get(self._sem.num, 0)
            if c:
                self._eng.wait_ge(self._sem, c)
            instr = a(*args, **kw)
            self._tk.cnt[self._sem.num] = c + 1
            instr.then_inc(self._sem, 1)
            return instr
        return wrap


def build_nc(T_total, debug_stage=3):
    n_tiles = T_total // TT
    nc = bass.Bass("TRN2", target_bir_lowering=False)
    tk = Trk()

    x_d = nc.dram_tensor("x", [D, T_total + TT], F32, kind="ExternalInput").ap()
    out_d = nc.dram_tensor("out", [D, T_total], F32, kind="ExternalOutput").ap()
    flag_d = nc.dram_tensor("flag", [128, 1], F32, kind="ExternalInput").ap()
    spx1_d = nc.dram_tensor("spx1", [D, T_total], F32).ap()
    spt1_d = nc.dram_tensor("spt1", [D, T_total + 1], BF16).ap()
    spnm_d = nc.dram_tensor("spnm", [1, T_total + 1], BF16).ap()
    spkk_d = nc.dram_tensor("spkk", [D, T_total], BF16).ap()
    spvv_d = nc.dram_tensor("spvv", [T_total, D], BF16).ap()
    spst_d = nc.dram_tensor("spst", [T_total // QC, 128, 512], F32).ap()
    ccin_d = nc.dram_tensor("ccin", [128, 512], F32)
    ccout_d = nc.dram_tensor("ccout", [2, 128, 512], F32)
    wr_d = nc.dram_tensor("WrT", [D, D], F32, kind="ExternalInput").ap()
    wk_d = nc.dram_tensor("WkT", [D, D], F32, kind="ExternalInput").ap()
    wv_d = nc.dram_tensor("WvT", [D, D], F32, kind="ExternalInput").ap()
    wo_d = nc.dram_tensor("WoT", [D, D], F32, kind="ExternalInput").ap()
    wd_d = nc.dram_tensor("WdT", [D, HD], F32, kind="ExternalInput").ap()
    wu_d = nc.dram_tensor("WuT", [HD, D], F32, kind="ExternalInput").ap()
    wg_d = nc.dram_tensor("wgate", [D, NE], F32, kind="ExternalInput").ap()
    bu_d = nc.dram_tensor("bu", [NE, D], F32, kind="ExternalInput").ap()
    bd_d = nc.dram_tensor("bd", [HD], F32, kind="ExternalInput").ap()
    br_d = nc.dram_tensor("br", [D], F32, kind="ExternalInput").ap()
    bk_d = nc.dram_tensor("bk", [D], F32, kind="ExternalInput").ap()
    bv_d = nc.dram_tensor("bv", [D], F32, kind="ExternalInput").ap()
    bo_d = nc.dram_tensor("bo", [D], F32, kind="ExternalInput").ap()
    mk_d = nc.dram_tensor("maak", [D], F32, kind="ExternalInput").ap()
    mv_d = nc.dram_tensor("maav", [D], F32, kind="ExternalInput").ap()
    mr_d = nc.dram_tensor("maar", [D], F32, kind="ExternalInput").ap()
    td_d = nc.dram_tensor("tdecay", [H], F32, kind="ExternalInput").ap()
    tf_d = nc.dram_tensor("tfaaaa", [H], F32, kind="ExternalInput").ap()

    ctx = contextlib.ExitStack()
    sb = lambda name, shape, dt: ctx.enter_context(nc.sbuf_tensor(name, shape, dt))
    ps = lambda name, shape: ctx.enter_context(nc.psum_tensor(name, shape, F32))
    sem = lambda name: ctx.enter_context(nc.semaphore(name))

    wr_s = sb("wr_s", [128, NKC, D], BF16)
    wk_s = sb("wk_s", [128, NKC, D], BF16)
    wv_s = sb("wv_s", [128, NKC, D], BF16)
    wo_s = sb("wo_s", [128, NKC, D], BF16)
    wd_s = sb("wd_s", [128, NKC, HD], BF16)
    wu_s = sb("wu_s", [128, 4, D], BF16)
    wg_s = sb("wg_s", [128, NKC, NE + 1], BF16)
    bu_s = sb("bu_s", [NE, D], BF16)
    bd_s = sb("bd_s", [128, 4], F32)
    bo_s = sb("bo_s", [128, NKC], F32)
    bvb_s = sb("bvb_s", [128, D], BF16)
    mkb_s = sb("mkb_s", [128, NKC], BF16)
    mvb_s = sb("mvb_s", [128, NKC], BF16)
    mrb_s = sb("mrb_s", [128, NKC], BF16)
    omkb_s = sb("omkb_s", [128, NKC], BF16)
    omvb_s = sb("omvb_s", [128, NKC], BF16)
    omrb_s = sb("omrb_s", [128, NKC], BF16)
    s12_s = {"k": sb("s12k_s", [2, D], BF16), "v": sb("s12v_s", [2, D], BF16),
             "r": sb("s12r_s", [2, D], BF16)}
    wdsum_s = sb("wdsum_s", [1, HD], BF16)
    bkr_s = sb("bkr_s", [1, D], BF16)
    brr_s = sb("brr_s", [1, D], BF16)
    onestt_s = sb("onestt_s", [1, TT], BF16)
    wgsb_s = sb("wgsb_s", [128, NE], F32)
    ind_s = sb("ind_s", [NE, 4, 128], BF16)
    identf = sb("identf", [128, 128], F32)
    identb = sb("identb", [128, 128], BF16)
    onesf = sb("onesf", [1, 128], F32)
    onesbr = sb("onesbr", [1, 128], BF16)
    onesb = sb("onesb", [128, 1], BF16)
    e01b = sb("e01b", [1, 4], BF16)
    wmt_s = sb("wmt_s", [128, H, QC], BF16)
    wbp_s = sb("wbp_s", [128, NE, QC], BF16)
    wktm_s = sb("wktm_s", [128, D], BF16)
    wsp_s = sb("wsp_s", [128, NE], F32)
    zb_s = sb("zb_s", [128, H], F32)
    ub_s = sb("ub_s", [128, H], F32)



    xres_s = sb("xres_s", [128, 2, TT], F32)
    x1_s = sb("x1_s", [128, NKC, TT], F32)
    scr2_s = sb("scr2_s", [128, 512], F32)
    iof_s = scr2_s[:, 0:128]
    iog_s = scr2_s[:, 128:256]
    iod_s = scr2_s[:, 256:384]
    iorow_s = sb("iorow_s", [NE, 2, 64], F32)
    ioh_s = [iorow_s[:, :, :]] * 4
    rowa_s = sb("rowa_s", [1, 16], F32).ap()
    rowb_s = sb("rowb_s", [1, 16], F32).ap()
    nhalf_s = sb("nhalf_s", [1, 1], F32)
    xbf_s = sb("xbf_s", [128, NKC, TT], BF16)
    xsq_s = sb("xsq_s", [128, NKC, TT], BF16)
    xv_s = sb("xv_s", [128, NKC, TT], BF16)
    t1_s = sb("t1_s", [128, NKC, TT + 1], BF16)
    kk_s = sb("kk_s", [128, NKC, TT], BF16)
    rr_s = sb("rr_s", [128, NKC, TT], BF16)
    ktm2_s = rr_s.rearrange("p a b -> p (a b)")[:, 0:D]
    vv_s = sb("vv_s", [128, NW, D], BF16)
    ktm_s = sb("ktm_s", [128, D], BF16)
    kwm_s = ktm_s
    scb_s = ktm_s[:, 0:TT]
    rrw_s = ktm_s[:, :].rearrange("p (a b) -> p a b", a=NE)
    atw_s = sb("atw_s", [128, 2, 2 * QC], BF16)
    st_s = sb("st_s", [128, 2, NE * NH], F32)
    stb_s = sb("stb_s", [128, 2, NE * NH], BF16)
    sigma_s = xres_s[:, 0, :]
    sigd_s = xres_s[:, 1, :]
    flag_s = sb("flag_s", [128, 1], F32)
    gh_s = xv_s
    rb_s = sb("rb_s", [128, TT], F32)
    gfm_s = sb("gfm_s", [NE, TT], BF16)
    gs9_s = scr2_s[0:NE + 1, 0:TT]
    lt64_s = sb("lt64_s", [128, 36], BF16)
    rst8_s = sb("rst8_s", [128, 8], F32)
    gtm_s = sb("gtm_s", [128, 4, NE], F32)
    gw1_s = sb("gw1_s", [128, 4, NE], F32)
    gm1_s = sb("gm1_s", [128, 4, 1], F32)
    gm2_s = sb("gm2_s", [128, 4, 1], F32)
    gmk1_s = sb("gmk1_s", [128, 4, NE], BF16)
    gmk2_s = sb("gmk2_s", [128, 4, NE], BF16)
    gde_s = sb("gde_s", [128, 4, 1], F32)
    gg1_s = sb("gg1_s", [128, 4, 1], F32)
    gg2_s = sb("gg2_s", [128, 4, 1], F32)
    mrow_s = sb("mrow_s", [1, TT], F32)
    trow_s = sb("trow_s", [1, TT], F32)
    rsg_s = sb("rsg_s", [1, TT], F32)

    ngmr_s = sb("ngmr_s", [1, TT + 1], BF16)
    nm_s = sb("nm_s", [2, TT], BF16)
    negm_s = sb("negm_s", [1, TT], BF16).ap()

    PB = [ps(f"PB{k}", [128, 512]) for k in range(8)]
    p_proj = [PB[0].ap(), PB[1].ap(), PB[6].ap(), PB[7].ap()]
    p_gate = PB[2][0:NE + 1, :]
    p_row = PB[3][0:2, :]
    p_bc = PB[4].ap()
    p_sm = PB[5][:, 0:128]

    pd_sl = [PB[2][:, 0:NH], PB[3][:, 0:NH]]
    py_sl = [PB[4][:, 0:QC], PB[5][:, 0:QC]]

    sT = sem("sT")
    sV = sem("sV")
    sS = sem("sS")
    sG = sem("sG")
    dX = sem("dX")
    dXa = sem("dXa")
    dXb = sem("dXb")
    dW = sem("dW")
    dK = sem("dK")
    dK2 = sem("dK2")
    dKb = sem("dKb")
    dKb2 = sem("dKb2")
    dO = sem("dO")
    dSx = sem("dSx")
    dSt = sem("dSt")
    dSk = sem("dSk")
    dSv = sem("dSv")
    dSs = [sem("dSsa"), sem("dSsb")]
    dLx = sem("dLx")
    dLt = sem("dLt")
    dLn = sem("dLn")
    dLk = sem("dLk")
    dLv = sem("dLv")
    dLs = [sem("dLsa"), sem("dLsb")]
    dC = sem("dC")

    tk.auto.update({sV.num, sS.num, sG.num})
    EMITS = []
    FIDX = [0]
    mock = MockEng()
    mock_s = SEng(mock, sS, tk)
    mock_v = SEng(mock, sV, tk)
    mock_g = SEng(mock, sG, tk)

    def _reg(key, f, wrap):
        idx = FIDX[0]
        FIDX[0] += 1
        EMITS.append((key, idx, f))
        tk.cur = idx
        f(wrap)

    def SY(f):
        _reg("y", f, mock)

    def TE(f):
        _reg("t", f, mock)

    def SC(f):
        _reg("s", f, mock_s)

    def VE(f):
        _reg("v", f, mock_v)

    def GP(f):
        _reg("g", f, mock_g)

    free = {"xbf": None, "xsq": None, "p_gate": None, "p_row": None, "p_bc": None,
            "p_sm": None, "pp": [None, None], "pw": [None, None],
            "pwy": [None, None], "pwd": [None, None],
            "ps4": [None, None, None, None],
            "negm": None, "rsg": None, "trow": None,
            "nm": None, "ngmr": None, "r2s": None, "gs": None, "gfm": None,
            "gtm": None, "hb": [None, None]}

    def _sy(sync):
        for d, s in ((bd_d.rearrange("(c p) -> p c", p=128), bd_s),
                     (bo_d.rearrange("(c p) -> p c", p=128), bo_s)):
            tk.inc(dX, sync.dma_start(out=s[:, :], in_=d,
                                      allow_slow_non_contiguous=True), step=16)
        tk.inc(dX, sync.dma_start(out=rowa_s, in_=td_d[None, :]), "td_row", 16)
        tk.inc(dX, sync.dma_start(out=rowb_s, in_=tf_d[None, :]), "tf_row", 16)
        tk.inc(dX, sync.dma_start(out=flag_s[:, :], in_=flag_d), "flag", 16)
        for j, dsem in ((0, dXa), (1, dXb)):
            tk.inc(dsem, sync.dma_start(
                out=xres_s[:, j, :],
                in_=x_d.rearrange("(c p) T -> p c T", p=128)[:, j, 0:TT]),
                f"xr:-1:{j}", 16)
        tk.mark(dX, "smalls")
        tk.ev["td_row"] = tk.ev["smalls"]
        tk.ev["tf_row"] = tk.ev["smalls"]
    SY(_sy)

    def _gw(g):
        for d, s in ((wr_d, wr_s), (wk_d, wk_s), (wv_d, wv_s), (wo_d, wo_s),
                     (wd_d, wd_s), (wu_d, wu_s)):
            tk.inc(dW, g.dma_start(out=s[:, :, :],
                                   in_=d.rearrange("(c p) o -> p c o", p=128)), step=16)
        tk.inc(dW, g.dma_start(out=wg_s[:, :, 1:NE + 1],
                               in_=wg_d.rearrange("(c p) o -> p c o", p=128)), step=16)
        tk.inc(dW, g.dma_start(out=bu_s[:, :], in_=bu_d), step=16)
        tk.inc(dW, g.dma_start(out=mkb_s[:, :], in_=mk_d.rearrange("(c p) -> p c", p=128), allow_slow_non_contiguous=True), step=16)
        tk.inc(dW, g.dma_start(out=mvb_s[:, :], in_=mv_d.rearrange("(c p) -> p c", p=128), allow_slow_non_contiguous=True), step=16)
        tk.inc(dW, g.dma_start(out=mrb_s[:, :], in_=mr_d.rearrange("(c p) -> p c", p=128), allow_slow_non_contiguous=True), step=16)
        tk.inc(dW, g.dma_start(out=bvb_s[0:1, :], in_=bv_d[None, :]), "bv_row", 16)
        tk.inc(dW, g.dma_start(out=bkr_s[:, :], in_=bk_d[None, :]), step=16)
        tk.inc(dW, g.dma_start(out=brr_s[:, :], in_=br_d[None, :]), step=16)
        tk.inc(dW, g.dma_start(
            out=xbf_s[:, :, :],
            in_=x_d.rearrange("(c p) T -> p c T", p=128)[:, :, 0:TT]), "f1_cast:-1", 16)
        tk.mark(dW, "weights")
        tk.ev["bv_row"] = tk.ev["weights"]
        tk.inc(sG, g.iota(out=iof_s, pattern=[[1, 128]], base=-1,
                          channel_multiplier=-1, allow_small_or_imprecise_dtypes=True),
               "io_d1")
        tk.inc(sG, g.tensor_scalar(out=iod_s, in0=iof_s, scalar1=0.0, scalar2=None,
                                   op0=ALU.max), "io_d1c")
        tk.inc(sG, g.iota(out=iog_s, pattern=[[1, 128]], base=0,
                          channel_multiplier=0, allow_small_or_imprecise_dtypes=True),
               "io_fr")
        for m in range(4):
            if m > 0:
                tk.wait(g, f"ind{m - 1}")
            tk.inc(sG, g.iota(out=ioh_s[m], pattern=[[-1, 2], [0, 64]],
                              base=-2 * m, channel_multiplier=1,
                              allow_small_or_imprecise_dtypes=True), f"io_ind{m}")
            tk.wait(g, f"io_ind{m}")
            tk.inc(sG, g.tensor_scalar(out=ind_s[:, m, :].rearrange("p (a b) -> p a b", a=2),
                                       in0=ioh_s[m], scalar1=0.0, scalar2=None,
                                       op0=ALU.is_equal), f"ind{m}")
    GP(_gw)

    def _v1(v):
        tk.wait(v, "io_d1")
        tk.inc(sV, v.tensor_scalar(out=identf[:, :], in0=iof_s, scalar1=-1.0,
                                   scalar2=None, op0=ALU.is_equal), "identf")
        tk.inc(sV, v.tensor_scalar(out=identb[:, :], in0=iof_s, scalar1=-1.0,
                                   scalar2=None, op0=ALU.is_equal), "identb")
        v.memset(onesf[:, :], 1.0)
        v.memset(onesbr[:, :], 1.0)
        v.memset(onesb[:, :], 1.0)
        v.memset(e01b[:, :], 0.0)
        v.memset(e01b[0:1, 0:1], 1.0)
        v.memset(e01b[0:1, 3:4], 1.0)
        v.memset(onestt_s[:, :], 1.0)
        v.memset(nhalf_s[:, :], -0.5)
        v.memset(st_s[:, :, :], 0.0)
        v.memset(stb_s[:, :, :], 0.0)
        v.memset(t1_s[:, :, 0:1], 0.0)
        tk.inc(sV, v.memset(ngmr_s[:, :], 0.0), "consts")
        tk.wait(v, "weights")
        v.memset(wg_s[:, :, 0:1], 1.0)
        v.tensor_scalar(out=omkb_s[:, :], in0=mkb_s[:, :], scalar1=-1.0, scalar2=1.0,
                        op0=ALU.mult, op1=ALU.add)
        v.tensor_scalar(out=omvb_s[:, :], in0=mvb_s[:, :], scalar1=-1.0, scalar2=1.0,
                        op0=ALU.mult, op1=ALU.add)
        tk.inc(sV, v.tensor_scalar(out=omrb_s[:, :], in0=mrb_s[:, :], scalar1=-1.0,
                                   scalar2=1.0, op0=ALU.mult, op1=ALU.add), "omaa")
    VE(_v1)

    def _s1(s):
        tk.wait(s, "td_row")
        tk.inc(sS, s.activation(out=rowa_s, in_=rowa_s, func=AF.Exp),
               "exp_td")
    SC(_s1)

    def _v2(v):
        tk.wait(v, "exp_td")
        tk.inc(sV, v.tensor_scalar(out=rowa_s, in0=rowa_s, scalar1=-1.0,
                                   scalar2=None, op0=ALU.mult), "z_row")
    VE(_v2)

    def _t1(t):
        tk.wait(t, "z_row")
        tk.wait(t, "tf_row")
        t.matmul(p_sm[:, 0:16], onesf[:, :], rowa_s, start=True, stop=True)
        tk.inc(sT, t.matmul(p_sm[:, 16:32], onesf[:, :], rowb_s, start=True,
                            stop=True), "zbub_mm")
    TE(_t1)

    def _v3(v):
        tk.wait(v, "zbub_mm")
        tk.inc(sV, v.tensor_copy(out=zb_s[:, :], in_=p_sm[:, 0:16]), "zb")
        tk.inc(sV, v.tensor_copy(out=ub_s[:, :], in_=p_sm[:, 16:32]), "ub")
    VE(_v3)

    for h in range(H):
        scr = rb_s[:, (h % 2) * QC:(h % 2) * QC + QC]
        def _se(s, h=h, scr=scr):
            if h == 0:
                tk.wait(s, "zb")
            if h >= 2:
                tk.wait(s, f"wmt{h - 2}")
            if h == 0:
                tk.wait(s, "io_d1c")
            tk.inc(sS, s.activation(out=scr, in_=iod_s, func=AF.Exp,
                                    scale=zb_s[:, h:h + 1]), f"wmexp{h}")
        SC(_se)
        def _ge(g, h=h, scr=scr):
            tk.wait(g, f"wmexp{h}")
            tk.inc(sG, g.affine_select(out=scr, in_=scr, pattern=[[1, QC]],
                                       compare_op=ALU.is_ge, fill=0.0, base=-1,
                                       channel_multiplier=-1), f"wmmask{h}")
        GP(_ge)
        def _ve(v, h=h, scr=scr):
            tk.wait(v, f"wmmask{h}")
            if h == 0:
                tk.wait(v, "ub")
                tk.wait(v, "identf")
            tk.inc(sV, v.scalar_tensor_tensor(out=wmt_s[:, h, :], in0=identf[:, :],
                                              scalar=ub_s[:, h:h + 1], in1=scr,
                                              op0=ALU.mult, op1=ALU.add), f"wmt{h}")
        VE(_ve)

    def _vz(v):
        tk.wait(v, "zb")
        tk.inc(sV, v.tensor_scalar(out=ub_s[:, :], in0=zb_s[:, :], scalar1=float(QC),
                                   scalar2=None, op0=ALU.mult), "zbq")
    def _vz2(v):
        tk.wait(v, "zb")
        tk.wait(v, f"wmt{H - 1}")
        tk.inc(sV, v.tensor_scalar(out=ub_s[:, :], in0=zb_s[:, :], scalar1=float(QC),
                                   scalar2=None, op0=ALU.mult), "zbq")
    VE(_vz2)
    def _s2(s):
        tk.wait(s, "io_fr")
        tk.wait(s, "zbq")
        for m in range(NE):
            for half, h in ((0, 2 * m), (1, 2 * m + 1)):
                rows = slice(half * 64, half * 64 + 64)
                s.activation(out=wbp_s[rows, m, :], in_=iog_s[rows, 0:QC], func=AF.Exp,
                             scale=zb_s[rows, h:h + 1])
                s.activation(out=wsp_s[rows, m:m + 1], in_=ub_s[rows, h:h + 1],
                             func=AF.Exp)
        tk.inc(sS, s.activation(out=rowb_s, in_=rowb_s, func=AF.Copy),
               "wbp")
    SC(_s2)

    def _g2(g):
        tk.wait(g, "wbp")
        tk.inc(sG, g.iota(out=iog_s, pattern=[[0, 128]], base=QC - 1,
                          channel_multiplier=-1, allow_small_or_imprecise_dtypes=True),
               "io_p127")
    GP(_g2)

    def _s3(s):
        tk.wait(s, "io_p127")
        for h in range(H):
            s.activation(out=wktm_s[:, h * NH:(h + 1) * NH], in_=iog_s[:, 0:NH],
                         func=AF.Exp, scale=zb_s[:, h:h + 1])
        tk.inc(sS, s.activation(out=rowb_s, in_=rowb_s, func=AF.Copy),
               "wkv_tables")
    SC(_s3)

    for name, w_s in (("k", wk_s), ("v", wv_s), ("r", wr_s)):
        mb = {"k": mkb_s, "v": mvb_s, "r": mrb_s}[name]
        omb = {"k": omkb_s, "v": omvb_s, "r": omrb_s}[name]
        for colh in range(2):
            tgn = f"{name}{colh}"
            def _ts(t, w_s=w_s, omb=omb, colh=colh, tgn=tgn):
                if tgn == "k0":
                    tk.wait(t, "omaa")
                else:
                    tk.wait(t, f"s12c_{ptgn}")
                cols = slice(colh * 512, colh * 512 + 512)
                for kc in range(NKC):
                    tk.inc(sT, t.matmul(p_row[0:1, :], omb[:, kc:kc + 1],
                                        w_s[:, kc, cols], start=(kc == 0),
                                        stop=(kc == NKC - 1)),
                           f"s12a_{tgn}" if kc == NKC - 1 else None)
            TE(_ts)
            def _vsa(v, tgn=tgn):
                tk.wait(v, f"s12a_{tgn}")
                tk.inc(sV, v.tensor_copy(out=negm_s[:, :], in_=p_row[0:1, :]),
                       f"s12ca_{tgn}")
            VE(_vsa)
            def _ts2(t, w_s=w_s, mb=mb, colh=colh, tgn=tgn):
                tk.wait(t, f"s12ca_{tgn}")
                cols = slice(colh * 512, colh * 512 + 512)
                for kc in range(NKC):
                    tk.inc(sT, t.matmul(p_row[0:1, :], mb[:, kc:kc + 1],
                                        w_s[:, kc, cols], start=(kc == 0),
                                        stop=(kc == NKC - 1)),
                           f"s12b_{tgn}" if kc == NKC - 1 else None)
            TE(_ts2)
            def _vsb(v, tgn=tgn):
                tk.wait(v, f"s12b_{tgn}")
                tk.inc(sV, v.tensor_copy(out=nm_s[0:1, :], in_=p_row[0:1, :]),
                       f"s12cb_{tgn}")
            VE(_vsb)
            def _ts3(t, tgn=tgn):
                tk.wait(t, f"s12cb_{tgn}")
                t.matmul(p_row[:, :], e01b[:, 0:2], negm_s[:, :], start=True, stop=False)
                tk.inc(sT, t.matmul(p_row[:, :], e01b[:, 2:4], nm_s[0:1, :],
                                    start=False, stop=True), f"s12p_{tgn}")
            TE(_ts3)
            def _vsc(v, colh=colh, name=name, tgn=tgn):
                tk.wait(v, f"s12p_{tgn}")
                cols = slice(colh * 512, colh * 512 + 512)
                tk.inc(sV, v.tensor_copy(out=s12_s[name][:, cols], in_=p_row[:, :]),
                       f"s12c_{tgn}")
            VE(_vsc)
            ptgn = tgn

    def _t2(t):
        tk.wait(t, "s12c_r1")
        for kc in range(NKC):
            tk.inc(sT, t.matmul(p_row[0:1, :], onesb[:, :], wd_s[:, kc, :],
                                start=(kc == 0), stop=(kc == NKC - 1)),
                   "wdsum_mm" if kc == NKC - 1 else None)
        for kc in range(NKC):
            tk.inc(sT, t.matmul(p_sm[0:1, 0:NE], onesb[:, :], wg_s[:, kc, 1:NE + 1],
                                start=(kc == 0), stop=(kc == NKC - 1)),
                   "wgsum_mm" if kc == NKC - 1 else None)
    TE(_t2)

    def _v4(v):
        tk.wait(v, "wdsum_mm")
        tk.inc(sV, v.tensor_copy(out=wdsum_s[:, :], in_=p_row[0:1, :]), "wdsum")
        tk.wait(v, "wgsum_mm")
        tk.inc(sV, v.tensor_copy(out=rowa_s[0:1, 0:NE], in_=p_sm[0:1, 0:NE]),
               "wgsum_row")
    VE(_v4)

    def _t3(t):
        tk.wait(t, "wgsum_row")
        tk.inc(sT, t.matmul(p_sm[:, 32:32 + NE], onesf[:, :], rowa_s[0:1, 0:NE],
                            start=True, stop=True), "wgsb_mm")
    TE(_t3)

    def _v5(v):
        tk.wait(v, "wgsb_mm")
        tk.inc(sV, v.tensor_copy(out=wgsb_s[:, :], in_=p_sm[:, 32:32 + NE]), "wgsb")
    VE(_v5)

    for half in range(2):
        def _tb(t, half=half):
            if half == 0:
                tk.wait(t, "consts")
                tk.wait(t, "bv_row")
            else:
                tk.wait(t, "bvbc0")
            tk.inc(sT, t.matmul(p_bc[:, :], onesbr[:, :],
                                bvb_s[0:1, half * 512:half * 512 + 512],
                                start=True, stop=True), f"bvbm{half}")
        TE(_tb)
        def _vb(v, half=half):
            tk.wait(v, f"bvbm{half}")
            tk.inc(sV, v.tensor_copy(out=bvb_s[:, half * 512:half * 512 + 512],
                                     in_=p_bc[:, :]), f"bvbc{half}")
        VE(_vb)

    for s_, nm_ in ((sT, "IT"), (sV, "IV"), (sS, "IS"), (sG, "IG"), (dX, "IDX"),
                    (dW, "IDW")):
        tk.mark(s_, nm_)
    init_marks = ["IT", "IV", "IS", "IG", "IDX", "IDW"]

    def _bar(e):
        for k in init_marks:
            tk.wait(e, k)
    SY(_bar); SC(_bar); VE(_bar); TE(_bar); GP(_bar)

    def pp_acquire(t, slot):
        tk.wait_opt(t, free["ps4"][slot])

    def ln_stats(i, tag, src, with_gate, src_ev):
        half = NKC // 2
        if src is not None:
            def _v(v):
                tk.wait_opt(v, src_ev)
                tk.wait_opt(v, free["xbf"])
                tk.inc(sV, v.tensor_copy(out=xbf_s[:, 0:half, :],
                                         in_=src[:, 0:half, :]),
                       f"{tag}_cast0:{i}")
                tk.inc(sV, v.tensor_copy(out=xbf_s[:, half:, :],
                                         in_=src[:, half:, :]),
                       f"{tag}_cast:{i}")
            VE(_v)
        def _s(s):
            if src is not None:
                tk.wait(s, f"{tag}_cast0:{i}")
                tk.wait_opt(s, free["xsq"])
                tk.inc(sS, s.activation(out=xsq_s[:, 0:half, :],
                                        in_=xbf_s[:, 0:half, :],
                                        func=AF.Square), f"{tag}_sq0:{i}")
                tk.wait(s, f"{tag}_cast:{i}")
                tk.inc(sS, s.activation(out=xsq_s[:, half:, :],
                                        in_=xbf_s[:, half:, :],
                                        func=AF.Square), f"{tag}_sq:{i}")
            else:
                tk.wait(s, f"{tag}_cast:{i}")
                tk.wait_opt(s, free["xsq"])
                tk.inc(sS, s.activation(out=xsq_s[:, :, :], in_=xbf_s[:, :, :],
                                        func=AF.Square), f"{tag}_sq:{i}")
        SC(_s)
        def _t(t):
            tk.wait(t, f"{tag}_cast0:{i}" if src is not None
                    else f"{tag}_cast:{i}")
            tk.wait_opt(t, free["p_gate"])
            if with_gate:
                for kc in range(NKC):
                    if kc == half and src is not None:
                        tk.wait(t, f"{tag}_cast:{i}")
                    tk.inc(sT, t.matmul(p_gate[:, :], wg_s[:, kc, :], xbf_s[:, kc, :],
                                        start=(kc == 0), stop=(kc == NKC - 1)),
                           f"{tag}_gmm:{i}" if kc == NKC - 1 else None)
            else:
                for kc in range(NKC):
                    if kc == half and src is not None:
                        tk.wait(t, f"{tag}_cast:{i}")
                    tk.inc(sT, t.matmul(p_gate[0:1, :], onesb[:, :], xbf_s[:, kc, :],
                                        start=(kc == 0), stop=(kc == NKC - 1)),
                           f"{tag}_gmm:{i}" if kc == NKC - 1 else None)
            tk.wait(t, f"{tag}_sq0:{i}" if src is not None else f"{tag}_sq:{i}")
            tk.wait_opt(t, free["p_row"])
            for kc in range(NKC):
                if kc == half and src is not None:
                    tk.wait(t, f"{tag}_sq:{i}")
                tk.inc(sT, t.matmul(p_row[0:1, :], onesb[:, :], xsq_s[:, kc, :],
                                    start=(kc == 0), stop=(kc == NKC - 1)),
                       f"{tag}_ssmm:{i}" if kc == NKC - 1 else None)
        TE(_t)
        def _v2(v):
            srow = p_gate[0:1, :]
            tk.wait(v, f"{tag}_gmm:{i}")
            tk.inc(sV, v.tensor_scalar(out=mrow_s[:, :], in0=srow, scalar1=1.0 / D,
                                       scalar2=None, op0=ALU.mult), f"{tag}_m:{i}")
            tk.wait(v, f"{tag}_ssmm:{i}")
            tk.wait_opt(v, free["trow"])
            v.tensor_tensor(out=trow_s[:, :], in0=mrow_s[:, :], in1=mrow_s[:, :],
                            op=ALU.mult)
            tk.wait_opt(v, free["rsg"])
            v.tensor_scalar(out=rsg_s[:, :], in0=p_row[0:1, :], scalar1=1.0 / D,
                            scalar2=EPS, op0=ALU.mult, op1=ALU.add)
            tk.inc(sV, v.tensor_tensor(out=trow_s[:, :], in0=rsg_s[:, :],
                                       in1=trow_s[:, :], op=ALU.subtract),
                   f"{tag}_var:{i}")
            tk.wait_opt(v, free["negm"])
            tk.inc(sV, v.tensor_scalar(out=negm_s[:, :], in0=mrow_s[:, :], scalar1=-1.0,
                                       scalar2=None, op0=ALU.mult), f"{tag}_negm:{i}")
        VE(_v2)
        free["p_row"] = f"{tag}_var:{i}"
        if not with_gate:
            free["p_gate"] = f"{tag}_m:{i}"
        def _s2(s):
            tk.wait(s, f"{tag}_var:{i}")
            tk.wait_opt(s, free["rsg"])
            s.activation(out=rsg_s[:, :], in_=trow_s[:, :], func=AF.Ln)
            tk.inc(sS, s.activation(out=rsg_s[:, :], in_=rsg_s[:, :], func=AF.Exp,
                                    scale=nhalf_s[0:1, 0:1]),
                   f"{tag}_rsig:{i}")
        SC(_s2)

    def emit_ffn(i, tag, src, dst, src_ev):
        ln_stats(i, tag, src, True, src_ev)
        def _v(v):
            tk.wait(v, f"{tag}_rsig:{i}")
            tk.inc(sV, v.tensor_tensor(out=trow_s[:, :], in0=mrow_s[:, :],
                                       in1=rsg_s[:, :], op=ALU.mult), f"{tag}_mr:{i}")
            tk.wait_opt(v, free["gs"])
            tk.inc(sV, v.tensor_copy(out=gs9_s[:, :], in_=p_gate[:, :]),
                   f"{tag}_gsc:{i}")
        VE(_v)
        free["p_gate"] = f"{tag}_gsc:{i}"
        for m in range(4):
            def _t6(t, m=m):
                tk.wait(t, f"{tag}_negm:{i}")
                pp_acquire(t, m % 4)
                for kc in range(NKC):
                    t.matmul(p_proj[m % 4][:, :], wd_s[:, kc, m * 128:(m + 1) * 128],
                             xbf_s[:, kc, :], start=(kc == 0), stop=False)
                tk.inc(sT, t.matmul(p_proj[m % 4][:, :],
                                    wdsum_s[:, m * 128:(m + 1) * 128], negm_s[:, :],
                                    start=False, stop=True), f"{tag}_hmm:{i}:{m}")
            TE(_t6)
            if m == 0:
                def _t4(t):
                    tk.wait(t, f"{tag}_rsig:{i}")
                    tk.wait_opt(t, free["p_bc"])
                    tk.inc(sT, t.matmul(p_bc[:, :], onesf[:, :], rsg_s[:, :],
                                        start=True, stop=True), f"{tag}_rbm:{i}")
                TE(_t4)
                def _v6(v):
                    tk.wait(v, f"{tag}_rbm:{i}")
                    tk.inc(sV, v.tensor_copy(out=rb_s[:, :], in_=p_bc[:, :]),
                           f"{tag}_rb:{i}")
                VE(_v6)
                free["p_bc"] = f"{tag}_rb:{i}"
                def _t2(t):
                    tk.wait(t, f"{tag}_mr:{i}")
                    tk.wait(t, f"{tag}_gsc:{i}")
                    tk.wait_opt(t, free["p_sm"])
                    for c in range(4):
                        t.transpose(p_sm[:, c * 9:c * 9 + NE + 1],
                                    gs9_s[:, c * 128:(c + 1) * 128],
                                    identf[0:NE + 1, 0:NE + 1])
                        t.matmul(p_sm[:, 64 + 2 * c:64 + 2 * c + 1],
                                 rsg_s[0:1, c * 128:(c + 1) * 128], onesf[0:1, 0:1],
                                 start=True, stop=True)
                        tk.inc(sT, t.matmul(p_sm[:, 65 + 2 * c:65 + 2 * c + 1],
                                            trow_s[0:1, c * 128:(c + 1) * 128],
                                            onesf[0:1, 0:1], start=True, stop=True),
                               f"{tag}_tpm:{i}" if c == 3 else None)
                TE(_t2)
                free["trow"] = f"{tag}_tpm:{i}"
                free["gs"] = f"{tag}_tpm:{i}"
            def _vh(v, m=m):
                tk.wait(v, f"{tag}_hmm:{i}:{m}")
                tk.wait(v, f"{tag}_rb:{i}") if m == 0 else None
                tk.inc(sV, v.tensor_tensor(out=gh_s[:, m, :],
                                           in0=p_proj[m % 4][:, :], in1=rb_s[:, :],
                                           op=ALU.mult), f"{tag}_h1:{i}:{m}")
                v.tensor_scalar(out=gh_s[:, m, :], in0=gh_s[:, m, :],
                                scalar1=bd_s[:, m:m + 1], scalar2=0.0,
                                op0=ALU.add, op1=ALU.max)
            VE(_vh)
            free["ps4"][m % 4] = f"{tag}_h1:{i}:{m}"
        def _v3(v):
            tk.wait(v, f"{tag}_tpm:{i}")
            tk.wait_opt(v, free["gtm"])
            v.tensor_copy(out=lt64_s[:, :], in_=p_sm[:, 0:36])
            tk.inc(sV, v.tensor_copy(out=rst8_s[:, :], in_=p_sm[:, 64:72]),
                   f"{tag}_ltm:{i}")
            ltv = lt64_s.rearrange("p (c x) -> p c x", x=9)[:, :, 1:1 + NE]
            rsv = rst8_s.rearrange("p (c x) -> p c x", x=2)
            v.tensor_tensor(out=gw1_s[:, :, :], in0=bc_last(rsv[:, :, 1:2], NE),
                            in1=bc_mid(wgsb_s[:, :], 4), op=ALU.mult)
            v.tensor_tensor(out=gtm_s[:, :, :], in0=ltv,
                            in1=bc_last(rsv[:, :, 0:1], NE), op=ALU.mult)
            v.tensor_tensor(out=gtm_s[:, :, :], in0=gtm_s[:, :, :], in1=gw1_s[:, :, :],
                            op=ALU.subtract)
            v.reduce_max(out=gm1_s[:, :, :], in_=gtm_s[:, :, :], axis=AX.X)
            v.tensor_tensor(out=gmk1_s[:, :, :], in0=gtm_s[:, :, :],
                            in1=bc_last(gm1_s[:, :, :], NE), op=ALU.is_ge)
            v.scalar_tensor_tensor(out=gw1_s[:, :, :], in0=gmk1_s[:, :, :],
                                   scalar=-1e30, in1=gtm_s[:, :, :], op0=ALU.mult,
                                   op1=ALU.add)
            v.reduce_max(out=gm2_s[:, :, :], in_=gw1_s[:, :, :], axis=AX.X)
            v.tensor_tensor(out=gmk2_s[:, :, :], in0=gw1_s[:, :, :],
                            in1=bc_last(gm2_s[:, :, :], NE), op=ALU.is_ge)
            tk.inc(sV, v.tensor_tensor(out=gde_s[:, :, :], in0=gm2_s[:, :, :],
                                       in1=gm1_s[:, :, :], op=ALU.subtract),
                   f"{tag}_d:{i}")
        VE(_v3)
        free["p_sm"] = f"{tag}_ltm:{i}"
        def _s(s):
            tk.wait(s, f"{tag}_d:{i}")
            tk.inc(sS, s.activation(out=gde_s[:, :, :], in_=gde_s[:, :, :], func=AF.Exp),
                   f"{tag}_e:{i}")
        SC(_s)
        def _v4(v):
            tk.wait(v, f"{tag}_e:{i}")
            v.tensor_scalar(out=gg1_s[:, :, :], in0=gde_s[:, :, :], scalar1=1.0,
                            scalar2=None, op0=ALU.add)
            v.reciprocal(out=gg1_s[:, :, :], in_=gg1_s[:, :, :])
            v.scalar_tensor_tensor(out=gg2_s[:, :, :], in0=gde_s[:, :, :], scalar=0.25,
                                   in1=gg1_s[:, :, :], op0=ALU.mult, op1=ALU.mult)
            v.tensor_scalar(out=gg1_s[:, :, :], in0=gg2_s[:, :, :], scalar1=-1.0,
                            scalar2=0.25, op0=ALU.mult, op1=ALU.add)
            v.tensor_tensor(out=gmk2_s[:, :, :], in0=gmk2_s[:, :, :],
                            in1=bc_last(gg2_s[:, :, :], NE), op=ALU.mult)
            v.tensor_tensor(out=gmk1_s[:, :, :], in0=gmk1_s[:, :, :],
                            in1=bc_last(gg1_s[:, :, :], NE), op=ALU.mult)
            tk.inc(sV, v.tensor_tensor(out=gtm_s[:, :, :], in0=gmk1_s[:, :, :],
                                       in1=gmk2_s[:, :, :], op=ALU.add),
                   f"{tag}_gtm:{i}")
        VE(_v4)
        def _t3(t):
            tk.wait(t, f"{tag}_gtm:{i}")
            tk.wait_opt(t, free["p_bc"])
            for c in range(4):
                tk.inc(sT, t.transpose(p_bc[0:NE, c * 128:(c + 1) * 128],
                                       gtm_s[:, c, :], identf[:, :]),
                       f"{tag}_gfmm:{i}" if c == 3 else None)
        TE(_t3)
        free["gtm"] = f"{tag}_gfmm:{i}"
        def _v5(v):
            tk.wait(v, f"{tag}_gfmm:{i}")
            tk.wait_opt(v, free["gfm"])
            tk.inc(sV, v.tensor_copy(out=gfm_s[:, :], in_=p_bc[0:NE, :]),
                   f"{tag}_gfm:{i}")
        VE(_v5)
        free["p_bc"] = f"{tag}_gfm:{i}"
        for m in range(4):
            def _t5(t, m=m):
                tk.wait(t, f"{tag}_gfm:{i}") if m == 0 else None
                pp_acquire(t, 2 + m % 2)
                tk.inc(sT, t.matmul(p_proj[2 + m % 2][:, :], ind_s[:, m, :],
                                    gfm_s[:, :], start=True, stop=True),
                       f"{tag}_scm:{i}:{m}")
            TE(_t5)
            def _v7(v, m=m):
                tk.wait(v, f"{tag}_scm:{i}:{m}")
                tk.inc(sV, v.tensor_copy(out=scb_s, in_=p_proj[2 + m % 2][:, :]),
                       f"{tag}_scb:{i}:{m}")
                tk.inc(sV, v.tensor_tensor(out=gh_s[:, m, :], in0=gh_s[:, m, :],
                                           in1=scb_s, op=ALU.mult),
                       f"{tag}_hev:{i}:{m}")
            VE(_v7)
            free["ps4"][2 + m % 2] = f"{tag}_scb:{i}:{m}"
        free["xbf"] = f"{tag}_hmm:{i}:3"
        free["xsq"] = f"{tag}_ssmm:{i}"
        free["negm"] = f"{tag}_hmm:{i}:3"
        free["rsg"] = f"{tag}_tpm:{i}"
        for j in range(NKC):
            def _t7(t, j=j):
                tk.wait(t, f"{tag}_hev:{i}:3") if j == 0 else None
                pp_acquire(t, j % 4)
                for kc in range(4):
                    t.matmul(p_proj[j % 4][:, :], wu_s[:, kc, j * 128:(j + 1) * 128],
                             gh_s[:, kc, :], start=(kc == 0), stop=False)
                tk.inc(sT, t.matmul(p_proj[j % 4][:, :], bu_s[:, j * 128:(j + 1) * 128],
                                    gfm_s[:, :], start=False, stop=True),
                       f"{tag}_umm:{i}:{j}")
            TE(_t7)
            def _v8(v, j=j):
                tk.wait(v, f"{tag}_umm:{i}:{j}")
                if tag == "f1":
                    if j == 0 and i > 0:
                        tk.wait(v, f"spx1:{i - 1}")
                    tk.wait(v, f"xr:{i}:{j}")
                    rsrc = xres_s[:, j % 2, :]
                else:
                    rsrc = src[:, j, :]
                tk.inc(sV, v.tensor_tensor(out=dst[:, j, :], in0=p_proj[j % 4][:, :],
                                           in1=rsrc, op=ALU.add),
                       f"{tag}_x1:{i}:{j}")
            VE(_v8)
            free["ps4"][j % 4] = f"{tag}_x1:{i}:{j}"
            if tag == "f1" and j + 2 < NKC:
                def _syx(sync, j=j):
                    tk.wait(sync, f"{tag}_x1:{i}:{j}")
                    tk.inc(dXa if j % 2 == 0 else dXb, sync.dma_start(
                        out=xres_s[:, j % 2, :],
                        in_=x_d.rearrange("(c p) T -> p c T", p=128)[
                            :, j + 2, (i + 1) * TT:(i + 2) * TT]), f"xr:{i}:{j + 2}", 16)
                SY(_syx)
        free["gfm"] = f"{tag}_umm:{i}:{NKC - 1}"
        tk.ev[f"{tag}_done:{i}"] = tk.ev[f"{tag}_x1:{i}:{NKC - 1}"]

    xw = x_d.rearrange("(c p) T -> p c T", p=128)

    def off(i):
        return (i + 1) * TT

    for i in range(-1, n_tiles):
        pre = (i == -1)
        emit_ffn(i, "f1", None, x1_s, None)
        if i + 1 < n_tiles:
            def _sy2(sync, i=i):
                for j, dsem in ((0, dXa), (1, dXb)):
                    tk.wait(sync, f"f1_x1:{i}:{6 + j}")
                    tk.inc(dsem, sync.dma_start(
                        out=xres_s[:, j, :],
                        in_=xw[:, j, off(i + 1):off(i + 1) + TT]), f"xr:{i + 1}:{j}", 16)
            SY(_sy2)
        if not pre:
            def _syx1(sync, i=i):
                tk.wait(sync, f"f1_done:{i}")
                tk.inc(dSx, sync.dma_start(
                    out=spx1_d.rearrange("(c p) T -> p c T", p=128)[
                        :, :, i * TT:(i + 1) * TT],
                    in_=x1_s[:, :, :]), f"spx1:{i}", 16)
            SY(_syx1)
        ln_stats(i, "a", x1_s, False, f"f1_done:{i}")
        def _va(v, i=i):
            tk.wait(v, f"a_rsig:{i}")
            tk.inc(sV, v.tensor_tensor(out=trow_s[:, :], in0=mrow_s[:, :],
                                       in1=rsg_s[:, :], op=ALU.mult), f"a_mr:{i}")
            tk.wait_opt(v, free["ngmr"])
            if i > -1:
                v.tensor_copy(out=ngmr_s[:, 0:1], in_=ngmr_s[:, TT:TT + 1])
            if i == -1:
                v.tensor_scalar(out=ngmr_s[:, 1:TT + 1], in0=trow_s[:, :],
                                scalar1=-1.0, scalar2=None, op0=ALU.mult)
                tk.wait(v, "flag")
                tk.inc(sV, v.tensor_scalar(out=ngmr_s[:, TT:TT + 1],
                                           in0=ngmr_s[:, TT:TT + 1],
                                           scalar1=flag_s[0:1, 0:1], scalar2=None,
                                           op0=ALU.mult), f"a_ngmr:{i}")
            else:
                tk.inc(sV, v.tensor_scalar(out=ngmr_s[:, 1:TT + 1], in0=trow_s[:, :],
                                           scalar1=-1.0, scalar2=None, op0=ALU.mult),
                       f"a_ngmr:{i}")
        VE(_va)
        free["ngmr"] = f"a_spnm:{i}"
        def _ta(t, i=i, pre=pre):
            if not pre:
                tk.wait(t, f"a_ngmr:{i}")
                tk.wait_opt(t, free["p_row"])
                t.matmul(p_row[:, :], e01b[:, 0:2], ngmr_s[:, 1:TT + 1], start=True,
                         stop=False)
                tk.inc(sT, t.matmul(p_row[:, :], e01b[:, 2:4], ngmr_s[:, 0:TT],
                                    start=False, stop=True), f"a_nmm:{i}")
            if pre:
                tk.wait(t, f"a_rsig:{i}")
            tk.wait_opt(t, free["p_bc"])
            tk.inc(sT, t.matmul(p_bc[:, :], onesf[:, :], rsg_s[:, :], start=True,
                                stop=True), f"a_rbm:{i}")
        TE(_ta)
        free["rsg"] = f"a_rbm:{i}"
        def _va2(v, i=i, pre=pre):
            if not pre:
                tk.wait(v, f"a_nmm:{i}")
                tk.wait_opt(v, free["negm"])
                tk.inc(sV, v.tensor_copy(out=nm_s[:, :], in_=p_row[:, :]), f"a_nm:{i}")
            tk.wait(v, f"a_rbm:{i}")
            tk.inc(sV, v.tensor_copy(out=rb_s[:, :], in_=p_bc[:, :]), f"a_rb:{i}")
            if i > -1:
                tk.wait(v, f"a_spnm:{i - 1}")
                v.tensor_copy(out=t1_s[:, :, 0:1], in_=t1_s[:, :, TT:TT + 1])
            if pre:
                v.tensor_tensor(out=t1_s[:, :, 1:TT + 1], in0=x1_s[:, :, :],
                                in1=bc_mid(rb_s[:, :], NKC), op=ALU.mult)
                tk.inc(sV, v.tensor_scalar(out=t1_s[:, :, TT:TT + 1],
                                           in0=t1_s[:, :, TT:TT + 1],
                                           scalar1=flag_s[:, 0:1], scalar2=None,
                                           op0=ALU.mult), f"a_t1:{i}")
            else:
                tk.inc(sV, v.tensor_tensor(out=t1_s[:, :, 1:TT + 1], in0=x1_s[:, :, :],
                                           in1=bc_mid(rb_s[:, :], NKC), op=ALU.mult),
                       f"a_t1:{i}")
            if not pre:
                tk.wait(v, f"a_gmm:{i}")
                tk.inc(sV, v.tensor_tensor(out=xbf_s[:, :, :], in0=t1_s[:, :, 0:TT],
                                           in1=t1_s[:, :, 1:TT + 1], op=ALU.subtract),
                       f"a_diff:{i}")
        VE(_va2)
        if not pre:
            free["p_row"] = f"a_nm:{i}"
        free["p_bc"] = f"a_rb:{i}"
        free["xbf"] = f"a_diff:{i}" if not pre else free["xbf"]
        def _syt1(sync, i=i, pre=pre):
            tk.wait(sync, f"a_t1:{i}")
            if pre:
                tk.inc(dSt, sync.dma_start(
                    out=spt1_d.rearrange("(c p) T -> p c T", p=128)[:, :, 0:1],
                    in_=t1_s[:, :, TT:TT + 1],
                    allow_slow_non_contiguous=True), f"spt1:{i}", 16)
                tk.wait(sync, f"a_ngmr:{i}")
                tk.inc(dSt, sync.dma_start(out=spnm_d[:, 0:1],
                                          in_=ngmr_s[:, TT:TT + 1]), f"a_spnm:{i}", 16)
            else:
                tk.inc(dSt, sync.dma_start(
                    out=spt1_d.rearrange("(c p) T -> p c T", p=128)[
                        :, :, 1 + i * TT:1 + (i + 1) * TT],
                    in_=t1_s[:, :, 1:TT + 1]), f"spt1:{i}", 16)
                tk.wait(sync, f"a_ngmr:{i}")
                tk.inc(dSt, sync.dma_start(out=spnm_d[:, 1 + i * TT:1 + (i + 1) * TT],
                                          in_=ngmr_s[:, 1:TT + 1]), f"a_spnm:{i}", 16)
        SY(_syt1)
        if pre:
            tk.ev[f"a_spnm_pre"] = tk.ev[f"a_spnm:{i}"]
            def _gx0(g, i=i):
                tk.wait(g, f"a_diff_pre_guard") if False else None
                tk.wait(g, f"a_ssmm:{i}")
                tk.wait(g, f"a_gmm:{i}")
                tk.inc(dW, g.dma_start(
                    out=xbf_s[:, :, :],
                    in_=xw[:, :, off(i + 1):off(i + 1) + TT]), f"f1_cast:{i + 1}", 16)
            GP(_gx0)
            continue

        def mix(name, maa, dst_b, extra_ev, i=i):
            def _vm(v):
                tk.wait(v, f"a_diff:{i}")
                tk.wait_opt(v, extra_ev)
                for kc in range(NKC):
                    tk.inc(sV, v.scalar_tensor_tensor(
                        out=dst_b[:, kc, :], in0=xbf_s[:, kc, :],
                        scalar=maa[:, kc:kc + 1],
                        in1=t1_s[:, kc, 1:TT + 1], op0=ALU.mult, op1=ALU.add),
                        f"a_mx{name}:{i}" if kc == NKC - 1 else None)
            VE(_vm)

        mix("k", mkb_s[:, :], xsq_s, f"a_ssmm:{i}")
        free["xsq"] = None
        for m in range(NKC):
            def _tk2(t, m=m, i=i):
                tk.wait(t, f"a_mxk:{i}") if m == 0 else None
                tk.wait(t, f"a_nm:{i}") if m == 0 else None
                pp_acquire(t, m % 4)
                for kc in range(NKC):
                    t.matmul(p_proj[m % 4][:, :], wk_s[:, kc, m * 128:(m + 1) * 128],
                             xsq_s[:, kc, :], start=(kc == 0), stop=False)
                t.matmul(p_proj[m % 4][:, :],
                         s12_s["k"][:, m * 128:(m + 1) * 128], nm_s[:, :],
                         start=False, stop=False)
                tk.inc(sT, t.matmul(p_proj[m % 4][:, :],
                                    bkr_s[:, m * 128:(m + 1) * 128], onestt_s[:, :],
                                    start=False, stop=True), f"kmm:{i}:{m}")
            TE(_tk2)
            def _sk(s, m=m, i=i):
                if m == 0:
                    if i > 0:
                        tk.wait(s, f"ktp:{i - 1}:{NW - 1}")
                        tk.wait(s, f"spkk:{i - 1}")
                tk.wait(s, f"kmm:{i}:{m}")
                tk.inc(sS, s.activation(out=kk_s[:, m, :], in_=p_proj[m % 4][:, :],
                                        func=AF.Copy), f"kev:{i}:{m}")
            SC(_sk)
            free["ps4"][m % 4] = f"kev:{i}:{m}"
        def _sykk(sync, i=i):
            tk.wait(sync, f"kev:{i}:{NKC - 1}")
            tk.inc(dSk, sync.dma_start(
                out=spkk_d.rearrange("(c p) T -> p c T", p=128)[
                    :, :, i * TT:(i + 1) * TT],
                in_=kk_s[:, :, :]), f"spkk:{i}", 16)
        SY(_sykk)

        mix("v", mvb_s[:, :], xv_s, None)
        if i + 1 < n_tiles:
            def _gx1(g, i=i):
                tk.wait(g, f"a_mxv:{i}")
                tk.inc(dW, g.dma_start(
                    out=xbf_s[:, :, :],
                    in_=xw[:, :, off(i + 1):off(i + 1) + TT]), f"f1_cast:{i + 1}", 16)
            GP(_gx1)
        for idx in range(NKC):
            sub, half = idx // 2, idx % 2
            def _tv(t, idx=idx, sub=sub, half=half, i=i):
                tk.wait(t, f"a_mxv:{i}") if idx == 0 else None
                pp_acquire(t, idx % 4)
                for kc in range(NKC):
                    t.matmul(p_proj[idx % 4][:, :],
                             xv_s[:, kc, sub * 128:(sub + 1) * 128],
                             wv_s[:, kc, half * 512:half * 512 + 512],
                             start=(kc == 0), stop=False)
                tk.inc(sT, t.matmul(p_proj[idx % 4][:, :],
                                    nm_s[:, sub * 128:(sub + 1) * 128],
                                    s12_s["v"][:, half * 512:half * 512 + 512],
                                    start=False, stop=True), f"vmm:{i}:{idx}")
            TE(_tv)
            def _vv2(v, idx=idx, sub=sub, half=half, i=i):
                tk.wait(v, f"vmm:{i}:{idx}")
                if idx == 0 and i > 0:
                    tk.wait(v, f"spvv:{i - 1}")
                tk.inc(sV, v.tensor_tensor(
                    out=vv_s[:, sub, half * 512:half * 512 + 512],
                    in0=p_proj[idx % 4][:, :],
                    in1=bvb_s[:, half * 512:half * 512 + 512],
                    op=ALU.add), f"vev:{i}:{idx}")
            VE(_vv2)
            free["ps4"][idx % 4] = f"vev:{i}:{idx}"
        def _syvv(sync, i=i):
            tk.wait(sync, f"vev:{i}:{NKC - 1}")
            tk.inc(dSv, sync.dma_start(
                out=spvv_d.rearrange("(ti x) d -> ti x d", x=TT)[i, :, :]
                .rearrange("(s p) d -> p s d", p=128),
                in_=vv_s[:, :, :]), f"spvv:{i}", 16)
        SY(_syvv)

        for c in range(NW):
            cs = slice(c * QC, (c + 1) * QC)
            gc = i * NW + c
            buf = ktm_s if c % 2 == 0 else ktm2_s
            pgc = gc - 2
            slA, slB = (2 * c) % 4, (2 * c + 1) % 4
            def _tt(t, c=c, cs=cs, i=i, slA=slA, slB=slB):
                if c == 0:
                    tk.wait(t, f"kev:{i}:{NKC - 1}")
                pp_acquire(t, slA)
                pp_acquire(t, slB)
                for m in range(NKC):
                    sl = slA if m < 4 else slB
                    col = (m % 4) * 128
                    tk.inc(sT, t.matmul(p_proj[sl][:, col:col + 128],
                                        kk_s[:, m, cs], identb[:, :],
                                        start=True, stop=True),
                           f"ktp:{i}:{c}" if m == NKC - 1 else None)
            TE(_tt)
            def _vkt(v, c=c, i=i, buf=buf, pgc=pgc, slA=slA, slB=slB):
                tk.wait(v, f"ktp:{i}:{c}")
                if pgc >= 0:
                    tk.wait(v, f"d:{pgc // NW}:{pgc % NW}:{NE - 1}")
                tk.inc(sV, v.tensor_tensor(out=buf[:, 0:512],
                                           in0=p_proj[slA][:, :],
                                           in1=wktm_s[:, 0:512], op=ALU.mult),
                       f"kkwa:{i}:{c}")
                tk.inc(sV, v.tensor_tensor(out=buf[:, 512:1024],
                                           in0=p_proj[slB][:, :],
                                           in1=wktm_s[:, 512:1024], op=ALU.mult),
                       f"kkw:{i}:{c}")
            VE(_vkt)
            free["ps4"][slA] = f"kkwa:{i}:{c}"
            free["ps4"][slB] = f"kkw:{i}:{c}"
            def _syst(sync, gc=gc, i=i, c=c):
                if gc > 0:
                    tk.wait(sync, f"stu:{gc - 1}:{NE - 1}")
                tk.inc(dSs[gc % 2], sync.dma_start(out=spst_d[gc, :, :],
                                                   in_=st_s[:, gc % 2, :]),
                       f"spst:{gc}", 16)
            SY(_syst)
            pdb = PB[2].ap() if c % 2 == 0 else PB[3].ap()
            for m in range(NE):
                def _td(t, c=c, m=m, i=i, gc=gc, buf=buf, pdb=pdb):
                    if m == 0:
                        tk.wait_opt(t, free["pwd"][c % 2])
                        tk.wait_opt(t, free["p_gate"] if c % 2 == 0
                                    else free["p_row"])
                        tk.wait(t, f"kkw:{i}:{c}")
                        tk.wait(t, f"vev:{i}:{2 * c + 1}")
                    vca = vv_s[:, c, (2 * m) * NH:(2 * m) * NH + NH]
                    vcb = vv_s[:, c, (2 * m + 1) * NH:(2 * m + 1) * NH + NH]
                    t.matmul(pdb[0:64, m * NH:(m + 1) * NH],
                             buf[:, (2 * m) * NH:(2 * m) * NH + NH],
                             vca, start=True, stop=True, tile_position=(0, 0))
                    tk.inc(sT, t.matmul(
                        pdb[64:128, m * NH:(m + 1) * NH],
                        buf[:, (2 * m + 1) * NH:(2 * m + 1) * NH + NH],
                        vcb, start=True, stop=True, tile_position=(0, 64)),
                        f"d:{i}:{c}:{m}")
                TE(_td)
            def _vst(v, c=c, i=i, gc=gc, pdb=pdb):
                tk.wait(v, f"d:{i}:{c}:{NE - 1}")
                if gc >= 1:
                    tk.wait(v, f"spst:{gc - 1}")
                wsb = bass.AP(wsp_s[:, :].tensor, wsp_s[:, :].offset,
                              list(wsp_s[:, :].ap) + [[0, NH]])
                stn = st_s[:, (gc + 1) % 2, :]
                v.tensor_tensor(
                    out=stn.rearrange("p (m x) -> p m x", m=NE),
                    in0=st_s[:, gc % 2, :].rearrange("p (m x) -> p m x", m=NE),
                    in1=wsb, op=ALU.mult)
                tk.inc(sV, v.tensor_tensor(out=stn, in0=stn, in1=pdb[:, :],
                                           op=ALU.add), f"stu:{gc}:{NE - 1}")
            VE(_vst)
            free["pwd"][c % 2] = f"stu:{gc}:{NE - 1}"
        free["p_gate"] = f"stu:{i * NW + 2}:{NE - 1}"
        free["p_row"] = f"stu:{i * NW + 3}:{NE - 1}"
        free["xsq"] = f"kmm:{i}:{NKC - 1}"
        free["negm"] = f"vmm:{i}:{NKC - 1}"

    n_gc = n_tiles * NW
    free["p_gate"] = f"stu:{n_gc - 2}:{NE - 1}"
    free["p_row"] = f"stu:{n_gc - 1}:{NE - 1}"
    def _gcc(g):
        tk.wait(g, f"stu:{n_gc - 1}:{NE - 1}")
        tk.inc(dC, g.dma_start(out=ccin_d.ap(), in_=st_s[:, n_gc % 2, :]), step=16)
        g.wait_ge(dC, 16)
        g.collective_compute(
            "AllGather", ALU.bypass,
            replica_groups=[[0, 1], [2, 3], [4, 5], [6, 7]],
            ins=[ccin_d.ap().opt()], outs=[ccout_d.ap().opt()]).then_inc(sG, 1)
        tk.cnt[sG.num] = tk.cnt.get(sG.num, 0) + 1
        tk.mark(sG, "ccdone")
        tk.wait(g, "ccdone")
        tk.wait(g, f"f1_x1:{n_tiles - 1}:{NKC - 1}")
        tk.inc(dC, g.dma_start(out=sigma_s, in_=ccout_d[0, :, :]), "sigld", 16)
    GP(_gcc)
    def _vcc(v):
        tk.wait(v, "sigld")
        tk.wait(v, "flag")
        tk.inc(sV, v.tensor_scalar(out=sigd_s, in0=sigma_s, scalar1=flag_s[:, 0:1],
                                   scalar2=None, op0=ALU.mult), "sigf")
    VE(_vcc)

    for i in range(n_tiles):
        def _syl(sync, i=i):
            if i == 0:
                tk.wait(sync, f"a_spnm:{n_tiles - 1}")
            else:
                tk.wait(sync, f"b_mxr:{i - 1}")
            tk.inc(dLt, sync.dma_start(
                out=t1_s[:, :, 0:TT + 1],
                in_=spt1_d.rearrange("(c p) T -> p c T", p=128)[
                    :, :, i * TT:i * TT + TT + 1]), f"ldt1:{i}", 16)
            if i == 0:
                tk.wait(sync, f"a_spnm:{n_tiles - 1}")
            else:
                tk.wait(sync, f"b_nmm:{i - 1}")
            tk.inc(dLn, sync.dma_start(
                out=ngmr_s[:, 0:TT + 1],
                in_=spnm_d[:, i * TT:i * TT + TT + 1]), f"ldnm:{i}", 16)
            if i == 0:
                tk.wait(sync, f"spkk:{n_tiles - 1}")
                tk.wait(sync, f"ktp:{n_tiles - 1}:{NW - 1}")
            else:
                tk.wait(sync, f"at:{i - 1}:{NW - 1}:{NE - 1}")
            tk.inc(dLk, sync.dma_start(
                out=kk_s[:, :, :],
                in_=spkk_d.rearrange("(c p) T -> p c T", p=128)[
                    :, :, i * TT:(i + 1) * TT]), f"ldkk:{i}", 16)
            if i == 0:
                tk.wait(sync, f"spvv:{n_tiles - 1}")
                tk.wait(sync, f"d:{n_tiles - 1}:{NW - 1}:{NE - 1}")
            else:
                tk.wait(sync, f"y:{i - 1}:{NW - 1}:{NE - 1}")
            tk.inc(dLv, sync.dma_start(
                out=vv_s[:, :, :],
                in_=spvv_d.rearrange("(ti x) d -> ti x d", x=TT)[i, :, :]
                .rearrange("(s p) d -> p s d", p=128)), f"ldvv:{i}", 16)
            if i == 0:
                tk.wait(sync, f"a_t1:{n_tiles - 1}")
                tk.wait(sync, f"spx1:{n_tiles - 1}")
            else:
                tk.wait(sync, f"out:{i - 1}")
            tk.inc(dLx, sync.dma_start(
                out=x1_s[:, :, :],
                in_=spx1_d.rearrange("(c p) T -> p c T", p=128)[
                    :, :, i * TT:(i + 1) * TT]), f"ldx1:{i}", 16)
        SY(_syl)
        def _tb(t, i=i):
            tk.wait(t, f"ldnm:{i}")
            tk.wait_opt(t, free["p_row"])
            t.matmul(p_row[:, :], e01b[:, 0:2], ngmr_s[:, 1:TT + 1], start=True,
                     stop=False)
            tk.inc(sT, t.matmul(p_row[:, :], e01b[:, 2:4], ngmr_s[:, 0:TT],
                                start=False, stop=True), f"b_nmm:{i}")
        TE(_tb)
        def _vb(v, i=i):
            tk.wait(v, f"b_nmm:{i}")
            tk.wait_opt(v, free["negm"])
            tk.inc(sV, v.tensor_copy(out=nm_s[:, :], in_=p_row[:, :]), f"b_nm:{i}")
            tk.wait(v, f"ldt1:{i}")
            tk.wait_opt(v, free["xbf"])
            tk.inc(sV, v.tensor_tensor(out=xbf_s[:, :, :], in0=t1_s[:, :, 0:TT],
                                       in1=t1_s[:, :, 1:TT + 1], op=ALU.subtract),
                   f"b_diff:{i}")
            tk.wait_opt(v, free["xsq"])
            for kc in range(NKC):
                tk.inc(sV, v.scalar_tensor_tensor(
                    out=xsq_s[:, kc, :], in0=xbf_s[:, kc, :],
                    scalar=mrb_s[:, kc:kc + 1],
                    in1=t1_s[:, kc, 1:TT + 1], op0=ALU.mult, op1=ALU.add),
                    f"b_mxr:{i}" if kc == NKC - 1 else None)
        VE(_vb)
        free["p_row"] = f"b_nm:{i}"
        free["xbf"] = f"b_diff:{i}"
        for m in range(NKC):
            def _tr(t, m=m, i=i):
                tk.wait(t, f"b_mxr:{i}") if m == 0 else None
                pp_acquire(t, m % 4)
                for kc in range(NKC):
                    t.matmul(p_proj[m % 4][:, :], wr_s[:, kc, m * 128:(m + 1) * 128],
                             xsq_s[:, kc, :], start=(kc == 0), stop=False)
                t.matmul(p_proj[m % 4][:, :],
                         s12_s["r"][:, m * 128:(m + 1) * 128], nm_s[:, :],
                         start=False, stop=False)
                tk.inc(sT, t.matmul(p_proj[m % 4][:, :],
                                    brr_s[:, m * 128:(m + 1) * 128], onestt_s[:, :],
                                    start=False, stop=True), f"rmm:{i}:{m}")
            TE(_tr)
            def _sr(s, m=m, i=i):
                tk.wait(s, f"rmm:{i}:{m}")
                tk.inc(sS, s.activation(out=rr_s[:, m, :], in_=p_proj[m % 4][:, :],
                                        func=AF.Copy), f"rev:{i}:{m}")
            SC(_sr)
            free["ps4"][m % 4] = f"rev:{i}:{m}"
        free["xsq"] = f"rmm:{i}:{NKC - 1}"
        free["negm"] = f"rmm:{i}:{NKC - 1}"

        for c in range(NW):
            cs = slice(c * QC, (c + 1) * QC)
            gc = i * NW + c
            def _syst2(sync, gc=gc):
                if gc == 0:
                    tk.wait(sync, "sigld")
                if gc < 2:
                    lastp = n_gc - 1 if (n_gc - 1) % 2 == gc % 2 else n_gc - 2
                    tk.wait(sync, f"spst:{lastp}")
                if gc >= 2:
                    tk.wait(sync, f"y:{(gc - 2) // NW}:{(gc - 2) % NW}:{NE - 1}")
                tk.inc(dLs[gc % 2], sync.dma_start(out=st_s[:, gc % 2, :],
                                                   in_=spst_d[gc, :, :]),
                       f"ldst:{gc}", 16)
            SY(_syst2)
            def _vsc(v, gc=gc, i=i, c=c):
                tk.wait(v, f"ldst:{gc}")
                tk.wait(v, "sigf") if gc == 0 else None
                if gc >= 2:
                    tk.wait(v, f"y:{(gc - 2) // NW}:{(gc - 2) % NW}:{NE - 1}")
                tk.inc(sV, v.tensor_tensor(out=stb_s[:, gc % 2, :],
                                           in0=st_s[:, gc % 2, :], in1=sigd_s,
                                           op=ALU.add), f"stc:{gc}")
                wspb = bc_last(wsp_s[:, :, None] if False else
                               bass.AP(wsp_s[:, :].tensor, wsp_s[:, :].offset,
                                       list(wsp_s[:, :].ap) + [[0, NH]]), NH) \
                    if False else bass.AP(wsp_s[:, :].tensor, wsp_s[:, :].offset,
                                          list(wsp_s[:, :].ap) + [[0, NH]])
                tk.inc(sV, v.tensor_tensor(out=sigd_s, in0=sigd_s, in1=wspb,
                                           op=ALU.mult), f"sigu:{gc}")
            VE(_vsc)
            def _grr(g, c=c, cs=cs, i=i):
                tk.wait(g, f"rev:{i}:{NKC - 1}") if c == 0 else None
                if i > 0 or c > 0:
                    pc = (i, c - 1) if c > 0 else (i - 1, NW - 1)
                    tk.wait(g, "y:%d:%d:%d" % (pc[0], pc[1], NE - 1))
                tk.inc(sG, g.tensor_tensor(out=rrw_s[:, :, :], in0=rr_s[:, :, cs],
                                           in1=wbp_s[:, :, :], op=ALU.mult),
                       f"rrw:{i}:{c}")
            GP(_grr)
            def mk_tat(m, c=c, cs=cs, i=i):
                sl = m % 2
                atA = p_proj[2 + sl][:, 0:QC]
                atB = p_proj[sl][:, 0:QC]
                def _tat(t, c=c, cs=cs, m=m, atA=atA, atB=atB, i=i):
                    if c == 0 and m < 2:
                        tk.wait(t, f"rev:{i}:{NKC - 1}")
                        tk.wait(t, f"ldkk:{i}")
                    pp_acquire(t, 2 + m % 2)
                    pp_acquire(t, m % 2)
                    t.matmul(atA, kk_s[0:64, m, cs], rr_s[0:64, m, cs],
                             start=True, stop=True, tile_position=(0, 0))
                    tk.inc(sT, t.matmul(atB, kk_s[64:128, m, cs],
                                        rr_s[64:128, m, cs], start=True, stop=True,
                                        tile_position=(64, 0)), f"at:{i}:{c}:{m}")
                def _vw(v, c=c, m=m, atA=atA, atB=atB, i=i):
                    tk.wait(v, f"at:{i}:{c}:{m}")
                    v.tensor_tensor(out=atw_s[:, m % 2, 0:QC], in0=atA,
                                    in1=wmt_s[:, 2 * m, :], op=ALU.mult)
                    tk.inc(sV, v.tensor_tensor(out=atw_s[:, m % 2, QC:2 * QC],
                                               in0=atB, in1=wmt_s[:, 2 * m + 1, :],
                                               op=ALU.mult), f"atw:{i}:{c}:{m}")
                TE(_tat); VE(_vw)
                free["ps4"][2 + m % 2] = f"atw:{i}:{c}:{m}"
                free["ps4"][m % 2] = f"atw:{i}:{c}:{m}"
            mk_tat(0)
            mk_tat(1)
            for m in range(NE):
                sl = m % 2
                y_sl = py_sl[sl]
                def _ty(t, c=c, m=m, y_sl=y_sl, i=i, gc=gc):
                    tk.wait(t, f"atw:{i}:{c}:{m}")
                    tk.wait_opt(t, free["pwy"][m % 2])
                    if c == 0 and m < 2:
                        tk.wait_opt(t, free["p_bc"])
                        tk.wait_opt(t, free["p_sm"])
                    tk.wait(t, f"rrw:{i}:{c}") if m == 0 else None
                    tk.wait(t, f"ldvv:{i}") if m == 0 else None
                    tk.wait(t, f"stc:{gc}") if m == 0 else None
                    vca = vv_s[:, c, (2 * m) * NH:(2 * m) * NH + NH]
                    vcb = vv_s[:, c, (2 * m + 1) * NH:(2 * m + 1) * NH + NH]
                    stbv = stb_s[:, gc % 2, :]
                    t.matmul(y_sl[0:64, :], vca, atw_s[:, m % 2, 0:QC], start=True,
                             stop=False, tile_position=(0, 0))
                    t.matmul(y_sl[0:64, :], stbv[0:64, m * NH:(m + 1) * NH],
                             rrw_s[0:64, m, :], start=False, stop=True,
                             tile_position=(0, 0))
                    t.matmul(y_sl[64:128, :], vcb, atw_s[:, m % 2, QC:2 * QC],
                             start=True, stop=False, tile_position=(0, 64))
                    tk.inc(sT, t.matmul(y_sl[64:128, :],
                                        stbv[64:128, m * NH:(m + 1) * NH],
                                        rrw_s[64:128, m, :], start=False, stop=True,
                                        tile_position=(64, 64)), f"y:{i}:{c}:{m}")
                def _sw(s, c=c, cs=cs, m=m, y_sl=y_sl, i=i):
                    tk.wait(s, f"y:{i}:{c}:{m}")
                    tk.inc(sS, s.activation(out=rr_s[:, m, cs], in_=y_sl,
                                            func=AF.Copy), f"yev:{i}:{c}:{m}")
                TE(_ty); SC(_sw)
                free["pwy"][m % 2] = f"yev:{i}:{c}:{m}"
                if m + 2 < NE:
                    mk_tat(m + 2)

        for m in range(NKC):
            def _to(t, m=m, i=i):
                tk.wait(t, f"yev:{i}:{NW - 1}:{NE - 1}") if m == 0 else None
                tk.wait(t, f"ldx1:{i}") if m == 0 else None
                pp_acquire(t, m % 4)
                for kc in range(NKC):
                    tk.inc(sT, t.matmul(p_proj[m % 4][:, :],
                                        wo_s[:, kc, m * 128:(m + 1) * 128],
                                        rr_s[:, kc, :], start=(kc == 0),
                                        stop=(kc == NKC - 1)),
                           f"omm:{i}:{m}" if kc == NKC - 1 else None)
            TE(_to)
            def _vo(v, m=m, i=i):
                tk.wait(v, f"omm:{i}:{m}")
                tk.inc(sV, v.scalar_tensor_tensor(out=x1_s[:, m, :],
                                                  in0=p_proj[m % 4][:, :],
                                                  scalar=bo_s[:, m:m + 1],
                                                  in1=x1_s[:, m, :], op0=ALU.add,
                                                  op1=ALU.add), f"x2:{i}:{m}")
            VE(_vo)
            free["ps4"][m % 4] = f"x2:{i}:{m}"
        tk.ev[f"x2d:{i}"] = tk.ev[f"x2:{i}:{NKC - 1}"]
        emit_ffn(i, "f3", x1_s, x1_s, f"x2d:{i}")
        def _go(g, i=i):
            tk.wait(g, f"f3_done:{i}")
            tk.inc(dO, g.dma_start(
                out=out_d.rearrange("(c p) T -> p c T", p=128)[
                    :, :, i * TT:(i + 1) * TT],
                in_=x1_s[:, :, :]), f"out:{i}", 16)
        GP(_go)

    def _fin(g):
        g.wait_ge(dO, tk.cnt[dO.num])
    GP(_fin)

    cnt1 = dict(tk.cnt)
    tk.pass2 = True
    tk.cnt = {}
    with nc.Block() as blk:
        for key, attr in (("y", "sync"), ("s", "scalar"), ("v", "vector"),
                          ("t", "tensor"), ("g", "gpsimd")):
            def run_all(e, key=key):
                if key == "s":
                    wrap = SEng(e, sS, tk)
                elif key == "v":
                    wrap = SEng(e, sV, tk)
                elif key == "g":
                    wrap = SEng(e, sG, tk)
                else:
                    wrap = e
                for k2, idx, f in EMITS:
                    if k2 == key:
                        tk.cur = idx
                        f(wrap)
            getattr(blk, attr)(run_all)
    for k, v in cnt1.items():
        assert tk.cnt.get(k, 0) == v, (k, v, tk.cnt.get(k, 0))

    ctx.close()
    return nc


def _prep_inputs(inputs):
    f = np.ascontiguousarray
    x = np.asarray(inputs["L0_attn_x"], np.float32)
    B, T, _ = x.shape
    base = {
        "WrT": f(np.asarray(inputs["L0_attn_Wr"], np.float32).T),
        "WkT": f(np.asarray(inputs["L0_attn_Wk"], np.float32).T),
        "WvT": f(np.asarray(inputs["L0_attn_Wv"], np.float32).T),
        "WoT": f(np.asarray(inputs["L0_attn_Wo"], np.float32).T),
        "WdT": f(np.asarray(inputs["L0_ffn_Wd"], np.float32).transpose(1, 0, 2)
                 .reshape(D, HD)),
        "WuT": f(np.asarray(inputs["L0_ffn_Wu"], np.float32).reshape(HD, D)),
        "wgate": f(np.asarray(inputs["L0_ffn_w_gate"], np.float32)),
        "bu": f(np.asarray(inputs["L0_ffn_bu"], np.float32)),
        "bd": f(np.asarray(inputs["L0_ffn_bd"], np.float32).reshape(HD)),
        "br": f(np.asarray(inputs["L0_attn_br"], np.float32)),
        "bk": f(np.asarray(inputs["L0_attn_bk"], np.float32)),
        "bv": f(np.asarray(inputs["L0_attn_bv"], np.float32)),
        "bo": f(np.asarray(inputs["L0_attn_bo"], np.float32)),
        "maak": f(np.asarray(inputs["L0_attn_time_maa_k"], np.float32)),
        "maav": f(np.asarray(inputs["L0_attn_time_maa_v"], np.float32)),
        "maar": f(np.asarray(inputs["L0_attn_time_maa_r"], np.float32)),
        "tdecay": f(np.asarray(inputs["L0_attn_time_decay"], np.float32)),
        "tfaaaa": f(np.asarray(inputs["L0_attn_time_faaaa"], np.float32)),
    }
    return base, x, B, T


def _core_inputs(base, x, B, T, core):
    half_T = T // 2
    b = (core // 2) % B
    half = core % 2
    lo = half * half_T
    xb = x[b].T
    win = np.zeros((D, TT + half_T), np.float32)
    if half == 1:
        win[:, 0:TT] = xb[:, lo - TT:lo]
    win[:, TT:] = xb[:, lo:lo + half_T]
    m = dict(base)
    m["x"] = np.ascontiguousarray(win)
    m["flag"] = np.full((128, 1), float(half), np.float32)
    return m


def kernel(**inputs):
    base, x, B, T = _prep_inputs(inputs)
    half_T = T // 2
    nc = build_nc(half_T)
    in_maps = [_core_inputs(base, x, B, T, c) for c in range(8)]
    res = run_bass_kernel_spmd(nc, in_maps, core_ids=list(range(8)))
    out = np.empty((B, T, D), np.float32)
    for b in range(B):
        for half in range(2):
            c = b * 2 + half
            out[b, half * half_T:(half + 1) * half_T] = res.results[c]["out"].T
    return out
